# version 01
# baseline (speedup 1.0000x reference)
import numpy as np
import concourse.bass as bass
import concourse.mybir as mybir
from concourse.bass_utils import run_bass_kernel_spmd
from concourse.alu_op_type import AluOpType as OP

F32 = mybir.dt.float32
U8 = mybir.dt.uint8
AF = mybir.ActivationFunctionType
AX = mybir.AxisListType.X

L, DM, LQ = 4096, 1024, 2048
G, DH = 4, 256
WIN = (32.0, 64.0, 128.0, 256.0)
ESC = 2.0 / 64.0
NE, DI = 9, 2816
NIC = DI // 128
NEG = -1.0e18
EPS = 1e-5
TB3 = 512
ENGS = ("sync", "tensor", "vector", "scalar", "gpsimd")
ARENA_W = 52000


def build(phases=("1a", "1b", "3", "2"), debug=False):
    nc = bass.Bass("TRN2", target_bir_lowering=False)

    x_in = nc.dram_tensor("x", [L, DM], F32, kind="ExternalInput").ap()
    qcap_in = nc.dram_tensor("qcap", [128, 16], F32, kind="ExternalInput").ap()
    wqt_in = nc.dram_tensor("wqt", [DM, DM], F32, kind="ExternalInput").ap()
    wqkvt_in = nc.dram_tensor("wqkvt", [DM, 3 * DM], F32, kind="ExternalInput").ap()
    bqkv_kc_in = nc.dram_tensor("bqkv_kc", [128, 8], F32, kind="ExternalInput").ap()
    bqkv_qc_in = nc.dram_tensor("bqkv_qc", [128, 8], F32, kind="ExternalInput").ap()
    bqkv_vb_in = nc.dram_tensor("bqkv_vb", [128, DM], F32, kind="ExternalInput").ap()
    wopt_in = nc.dram_tensor("wopt", [DM, 256], F32, kind="ExternalInput").ap()
    w2t_in = nc.dram_tensor("w2t", [128, 2], F32, kind="ExternalInput").ap()
    b2c_in = nc.dram_tensor("b2c", [128, 1], F32, kind="ExternalInput").ap()
    ident_in = nc.dram_tensor("ident", [128, 128], F32, kind="ExternalInput").ap()
    dmats_in = nc.dram_tensor("dmats", [128, 5 * 512], F32, kind="ExternalInput").ap()
    wr_in = nc.dram_tensor("wr", [DM, NE], F32, kind="ExternalInput").ap()
    wg_in = nc.dram_tensor("wg", [NE, DM, DI], F32, kind="ExternalInput").ap()
    wu_in = nc.dram_tensor("wu", [NE, DM, DI], F32, kind="ExternalInput").ap()
    wd_in = nc.dram_tensor("wd", [NE, DI, DM], F32, kind="ExternalInput").ap()
    out_d = nc.dram_tensor("out", [LQ, DM], F32, kind="ExternalOutput").ap()

    skind = dict(kind="ExternalOutput") if debug else {}
    k_s = nc.dram_tensor("k_s", [DM, L], F32, **skind).ap()
    v_s = nc.dram_tensor("v_s", [L, DM], F32, **skind).ap()
    q_s = nc.dram_tensor("q_s", [DM, LQ], F32, **skind).ap()
    qin_s = nc.dram_tensor("qin_s", [DM, LQ], F32, **skind).ap()
    xn_s = nc.dram_tensor("xn_s", [DM, LQ], F32, **skind).ap()
    dbg = nc.dram_tensor("dbg", [128, 512], F32, kind="ExternalOutput").ap() if debug else None
    dbg2 = nc.dram_tensor("dbg2", [128, 4096], F32, kind="ExternalOutput").ap() if debug else None
    dbg3 = nc.dram_tensor("dbg3", [128, 4096], F32, kind="ExternalOutput").ap() if debug else None

    import contextlib
    es = contextlib.ExitStack()
    with es:
        arena_t = es.enter_context(nc.sbuf_tensor("arena", [128, ARENA_W], F32))
        msku = [es.enter_context(nc.sbuf_tensor(f"msku{i}", [128, 512], U8)) for i in range(2)]
        msk9 = es.enter_context(nc.sbuf_tensor("msk9", [128, 16], U8))
        ps = [es.enter_context(nc.psum_tensor(f"ps{i}", [128, 512], F32)) for i in range(8)]
        sem = {n: es.enter_context(nc.semaphore(f"s_{n}"))
               for n in ("ld", "st", "pe", "ve", "se", "ge")}

        ops = {e: [] for e in ENGS}
        cnt = {n: 0 for n in sem}

        def on(e, f):
            ops[e].append(f)

        def wait(e, s, v):
            if v > 0:
                on(e, lambda E, s=s, v=v: E.wait_ge(sem[s], v))

        def emit(e, builder, inc=None, incn=1):
            if inc is None:
                on(e, builder)
                return None
            cnt[inc] += incn
            on(e, lambda E, b=builder, s=inc, n=incn: b(E).then_inc(sem[s], n))
            return cnt[inc]

        OWN = {"vector": "ve", "scalar": "se", "gpsimd": "ge", "tensor": "pe"}

        def fence(e):
            wait(e, OWN[e], cnt[OWN[e]])

        def dma_in(dst, src):
            return emit("sync", lambda E, d=dst, s=src: E.dma_start(out=d, in_=s),
                        inc="ld", incn=16)

        def dma_out(dst, src, waits=()):
            for s, v in waits:
                wait("gpsimd", s, v)
            return emit("gpsimd", lambda E, d=dst, s=src: E.dma_start(out=d, in_=s),
                        inc="st", incn=16)

        def barrier():
            snap = dict(cnt)
            own = {"tensor": ("pe",), "vector": ("ve",), "scalar": ("se",),
                   "gpsimd": ("ge", "st"), "sync": ()}
            for e in ENGS:
                for s, v in snap.items():
                    if s not in own[e]:
                        wait(e, s, v)

        class Arena:
            def __init__(self):
                self.off = 0
                self.stack = []

            def alloc(self, w):
                w16 = (w + 15) // 16 * 16
                assert self.off + w16 <= ARENA_W, f"arena overflow: {self.off + w16}"
                ap = arena_t[:, self.off:self.off + w]
                self.off += w16
                return ap

            def mark(self):
                self.stack.append(self.off)

            def release(self):
                self.off = self.stack.pop()

        A = Arena()

        ident = A.alloc(128)
        dmats = A.alloc(5 * 512)
        negbig = A.alloc(512)
        qcap = A.alloc(16)
        b2c = A.alloc(16)[:, 0:1]
        w2t = A.alloc(16)
        ones = A.alloc(128)
        dxcols = A.alloc(16)
        sm = A.alloc(64)

        def S1(i):
            return sm[:, i:i + 1]

        XSUM, SUMSQ, MEAN, VAR, SD, RSIG, NB = 0, 1, 2, 3, 4, 5, 6
        MAXV, EBIAS, SUMV, RECIP, THRQ, THRK = 7, 8, 9, 10, 11, 12
        M1, M2, D12, W1, W2, SGT = 13, 14, 15, 16, 17, 18

        dma_in(ident, ident_in)
        dma_in(dmats, dmats_in)
        dma_in(qcap, qcap_in)
        dma_in(b2c, b2c_in)
        dma_in(w2t[:, 0:2], w2t_in[:, 0:2])
        emit("gpsimd", lambda E: E.memset(negbig[:, :], NEG), inc="ge")
        emit("gpsimd", lambda E: E.memset(ones[0:1, :], 1.0), inc="ge")
        consts_ld = cnt["ld"]

        bank_free = [("ve", 0)] * 8

        def mm_group(bank, pairs, wait_list=(), bump=True):
            s, v = bank_free[bank]
            wait("tensor", s, v)
            for ss, vv in wait_list:
                wait("tensor", ss, vv)
            n = len(pairs)
            for i, (lh, rh, outap) in enumerate(pairs):
                inc = "pe" if (bump and i == n - 1) else None
                emit("tensor",
                     lambda E, o=outap, l=lh, r=rh, st=(i == 0), sp=(i == n - 1):
                     E.matmul(o, l, r, start=st, stop=sp, skip_group_check=True),
                     inc=inc)
            return cnt["pe"]

        def transp(bank, in_ap, out_ap, wait_list=(), bump=True, bank_wait=True):
            if bank_wait:
                s, v = bank_free[bank]
                wait("tensor", s, v)
            for ss, vv in wait_list:
                wait("tensor", ss, vv)
            return emit("tensor",
                        lambda E, o=out_ap, i=in_ap: E.transpose(o, i, ident),
                        inc="pe" if bump else None)

        if "1a" in phases:
            A.mark()
            wqt = A.alloc(8 * DM)
            wqkvt = A.alloc(8 * 3 * DM)
            bqkv_kc = A.alloc(16)
            bqkv_qc = A.alloc(16)
            bqkv_vb = A.alloc(DM)
            x_t = [A.alloc(DM), A.alloc(DM)]
            xn_t = [A.alloc(DM), A.alloc(DM)]
            xn_fm = A.alloc(8 * 512)
            qin_fm = A.alloc(8 * 512)
            stg = [A.alloc(512), A.alloc(512)]

            dma_in(wqt.rearrange("p (c o) -> p c o", c=8),
                   wqt_in.rearrange("(c p) o -> p c o", p=128))
            dma_in(wqkvt.rearrange("p (c o) -> p c o", c=8),
                   wqkvt_in.rearrange("(c p) o -> p c o", p=128))
            dma_in(bqkv_kc[:, 0:8], bqkv_kc_in)
            dma_in(bqkv_qc[:, 0:8], bqkv_qc_in)
            dma_in(bqkv_vb, bqkv_vb_in)
            w_ld = cnt["ld"]
            wait("tensor", "ld", consts_ld)

            xbuf_free = [("se", 0), ("se", 0)]
            xnfm_dma_st = 0
            qinfm_dma_st = 0
            xnbuf_free = [("pe", 0), ("pe", 0)]
            stg_free = [("st", 0), ("st", 0)]
            stats_read = 0

            for tb in range(8):
                own = tb < 4
                for s4 in range(4):
                    t0 = tb * 512 + s4 * 128
                    bi = s4 % 2
                    xe, xne = x_t[bi], xn_t[bi]
                    ss, vv = xbuf_free[bi]
                    wait("sync", ss, vv)
                    l = dma_in(xe, x_in[t0:t0 + 128, :])
                    ss, vv = xnbuf_free[bi]
                    wait("scalar", ss, vv)
                    wait("scalar", "ld", l)
                    wait("scalar", "ve", stats_read)
                    emit("scalar", lambda E, xn=xne, x=xe:
                         E.activation(xn, x, AF.Square, accum_out=S1(SUMSQ)), inc="se")
                    se_sq = cnt["se"]
                    wait("vector", "ld", l)
                    emit("vector", lambda E, x=xe: E.reduce_sum(S1(XSUM), x, axis=AX), inc="ve")
                    fence("vector")
                    wait("vector", "se", se_sq)
                    emit("vector", lambda E: E.tensor_scalar(S1(MEAN), S1(XSUM), 1.0 / DM, None, OP.mult), inc="ve")
                    emit("vector", lambda E: E.tensor_scalar(S1(VAR), S1(SUMSQ), 1.0 / DM, None, OP.mult), inc="ve")
                    fence("vector")
                    emit("vector", lambda E: E.tensor_tensor(S1(SGT), S1(MEAN), S1(MEAN), OP.mult), inc="ve")
                    fence("vector")
                    emit("vector", lambda E: E.scalar_tensor_tensor(
                        S1(SGT), S1(SGT), -1.0, S1(VAR), OP.mult, OP.add), inc="ve")
                    fence("vector")
                    emit("vector", lambda E: E.tensor_scalar(S1(VAR), S1(SGT), EPS, None, OP.add), inc="ve")
                    ve_var = cnt["ve"]
                    wait("scalar", "ve", ve_var)
                    emit("scalar", lambda E: E.activation(S1(SD), S1(VAR), AF.Sqrt), inc="se")
                    se_sd = cnt["se"]
                    wait("vector", "se", se_sd)
                    emit("vector", lambda E: E.reciprocal(S1(RSIG), S1(SD)), inc="ve")
                    fence("vector")
                    emit("vector", lambda E: E.scalar_tensor_tensor(
                        S1(NB), S1(MEAN), -1.0, S1(RSIG), OP.mult, OP.mult), inc="ve")
                    fence("vector")
                    stats_read = cnt["ve"]
                    ss, vv = xnbuf_free[bi]
                    wait("vector", ss, vv)
                    emit("vector", lambda E, xn=xne, x=xe: E.tensor_scalar(
                        xn, x, S1(RSIG), S1(NB), OP.mult, OP.add), inc="ve")
                    ve_xn = cnt["ve"]
                    xbuf_free[bi] = ("ve", ve_xn)
                    if s4 == 0:
                        wait("vector", "st", xnfm_dma_st)
                    for grp in range(2):
                        bk = grp
                        wl = [("ve", ve_xn)]
                        for c4 in range(4):
                            c = grp * 4 + c4
                            transp(bk, xne[:, c * 128:(c + 1) * 128],
                                   ps[bk][:, c4 * 128:(c4 + 1) * 128],
                                   wait_list=wl if c4 == 0 else (),
                                   bump=(c4 == 3), bank_wait=(c4 == 0))
                        pe_t = cnt["pe"]
                        wait("vector", "pe", pe_t)
                        for c4 in range(4):
                            c = grp * 4 + c4
                            emit("vector", lambda E, c=c, c4=c4, bk=bk, s4=s4:
                                 E.tensor_copy(xn_fm[:, c * 512 + s4 * 128:c * 512 + s4 * 128 + 128],
                                               ps[bk][:, c4 * 128:(c4 + 1) * 128]),
                                 inc="ve" if c4 == 3 else None)
                        bank_free[bk] = ("ve", cnt["ve"])
                    xnbuf_free[bi] = ("pe", cnt["pe"])
                xnfm_ready = cnt["ve"]

                if own:
                    st_c = dma_out(xn_s.rearrange("(c p) t -> p c t", p=128)[:, :, tb * 512:(tb + 1) * 512],
                                   xn_fm.rearrange("p (c t) -> p c t", c=8),
                                   waits=[("ve", xnfm_ready)])
                    xnfm_dma_st = st_c

                wait("vector", "st", qinfm_dma_st)
                for oc in range(8):
                    bk = 2 + oc % 2
                    pairs = [(wqt[:, c * DM + oc * 128:c * DM + oc * 128 + 128],
                              xn_fm[:, c * 512:(c + 1) * 512], ps[bk][:, :]) for c in range(8)]
                    pe_g = mm_group(bk, pairs,
                                    wait_list=[("ve", xnfm_ready), ("ld", w_ld)])
                    wait("vector", "pe", pe_g)
                    emit("vector", lambda E, oc=oc, bk=bk:
                         E.tensor_copy(qin_fm[:, oc * 512:(oc + 1) * 512], ps[bk][:, :]), inc="ve")
                    bank_free[bk] = ("ve", cnt["ve"])
                qinfm_ready = cnt["ve"]
                if own:
                    qinfm_dma_st = dma_out(
                        qin_s.rearrange("(c p) t -> p c t", p=128)[:, :, tb * 512:(tb + 1) * 512],
                        qin_fm.rearrange("p (c t) -> p c t", c=8),
                        waits=[("ve", qinfm_ready)])

                dsts = [("k", 1024, bqkv_kc, k_s, L, tb * 512)]
                if own:
                    dsts.append(("q", 0, bqkv_qc, q_s, LQ, tb * 512))
                for name, obase, bias_c, dstt, dw, tcol in dsts:
                    for oc in range(8):
                        bk = 2 + oc % 2
                        pairs = [(wqkvt[:, c * 3 * DM + obase + oc * 128:
                                        c * 3 * DM + obase + oc * 128 + 128],
                                  qin_fm[:, c * 512:(c + 1) * 512], ps[bk][:, :])
                                 for c in range(8)]
                        pe_g = mm_group(bk, pairs,
                                        wait_list=[("ve", qinfm_ready), ("ld", w_ld)])
                        sb = oc % 2
                        ss, vv = stg_free[sb]
                        wait("vector", ss, vv)
                        wait("vector", "ld", w_ld)
                        wait("vector", "pe", pe_g)
                        emit("vector", lambda E, bk=bk, sb=sb, bias_c=bias_c, oc=oc:
                             E.tensor_scalar(stg[sb], ps[bk][:, :],
                                             bias_c[:, oc:oc + 1], None, OP.add), inc="ve")
                        bank_free[bk] = ("ve", cnt["ve"])
                        st_c = dma_out(dstt[oc * 128:(oc + 1) * 128, tcol:tcol + 512], stg[sb],
                                       waits=[("ve", cnt["ve"])])
                        stg_free[sb] = ("st", st_c)

                for s4 in range(4):
                    for half in range(2):
                        bk = 4 + half
                        pairs = [(qin_fm[:, c * 512 + s4 * 128:c * 512 + s4 * 128 + 128],
                                  wqkvt[:, c * 3 * DM + 2048 + half * 512:
                                        c * 3 * DM + 2048 + half * 512 + 512], ps[bk][:, :])
                                 for c in range(8)]
                        pe_g = mm_group(bk, pairs,
                                        wait_list=[("ve", qinfm_ready), ("ld", w_ld)])
                        sb = half
                        ss, vv = stg_free[sb]
                        wait("vector", ss, vv)
                        wait("vector", "ld", w_ld)
                        wait("vector", "pe", pe_g)
                        emit("vector", lambda E, bk=bk, sb=sb, half=half:
                             E.scalar_tensor_tensor(stg[sb], ps[bk][:, :], 1.0,
                                                    bqkv_vb[:, half * 512:half * 512 + 512],
                                                    OP.mult, OP.add), inc="ve")
                        bank_free[bk] = ("ve", cnt["ve"])
                        st_c = dma_out(v_s[tb * 512 + s4 * 128:tb * 512 + s4 * 128 + 128,
                                           half * 512:half * 512 + 512], stg[sb],
                                       waits=[("ve", cnt["ve"])])
                        stg_free[sb] = ("st", st_c)
            barrier()
            A.release()

        if "1b" in phases:
            A.mark()
            wopt = A.alloc(8 * 256)
            qin_b = A.alloc(8 * 512)
            relu_b = A.alloc(8 * 512)
            h_fm = A.alloc(2 * LQ)
            dma_in(wopt.rearrange("p (c o) -> p c o", c=8),
                   wopt_in.rearrange("(c p) o -> p c o", p=128))
            wop_ld = cnt["ld"]
            relu_read = 0
            for blk in range(4):
                wait("sync", "se", relu_read)
                l = dma_in(qin_b.rearrange("p (c t) -> p c t", c=8),
                           qin_s.rearrange("(c p) t -> p c t", p=128)[:, :, blk * 512:(blk + 1) * 512])
                wait("scalar", "ld", l)
                wait("scalar", "pe", cnt["pe"])
                emit("scalar", lambda E: E.activation(relu_b, qin_b, AF.Relu), inc="se")
                relu_read = cnt["se"]
                for oc in range(2):
                    bk = oc
                    pairs = [(wopt[:, c * 256 + oc * 128:c * 256 + oc * 128 + 128],
                              relu_b[:, c * 512:(c + 1) * 512], ps[bk][:, :]) for c in range(8)]
                    pe_g = mm_group(bk, pairs, wait_list=[("se", relu_read), ("ld", wop_ld)])
                    wait("vector", "pe", pe_g)
                    emit("vector", lambda E, oc=oc, blk=blk, bk=bk:
                         E.tensor_copy(h_fm[:, oc * LQ + blk * 512:oc * LQ + blk * 512 + 512],
                                       ps[bk][:, :]), inc="ve")
                    bank_free[bk] = ("ve", cnt["ve"])
            h_ready = cnt["ve"]
            for tc in range(16):
                bk = 2 + tc % 2
                pairs = [(h_fm[:, oc * LQ + tc * 128:oc * LQ + tc * 128 + 128],
                          w2t[:, oc:oc + 1], ps[bk][:, 0:1]) for oc in range(2)]
                pe_g = mm_group(bk, pairs, wait_list=[("ve", h_ready)])
                wait("scalar", "pe", pe_g)
                wait("scalar", "ve", cnt["ve"])
                emit("scalar", lambda E, bk=bk: E.activation(
                    S1(SGT), ps[bk][:, 0:1], AF.Sigmoid, bias=b2c, scale=1.0), inc="se")
                wait("vector", "se", cnt["se"])
                emit("vector", lambda E, tc=tc: E.tensor_scalar(
                    dxcols[:, tc:tc + 1], S1(SGT), float(L), None, OP.mult), inc="ve")
                bank_free[bk] = ("ve", cnt["ve"])
            barrier()
            A.release()

        A.mark()
        out_acc = A.alloc(8 * LQ)

        if "3" in phases:
            A.mark()
            hidden = A.alloc(NIC * TB3)
            xn_b = A.alloc(8 * 512)
            wg_sl = [A.alloc(8 * 128), A.alloc(8 * 128)]
            wu_sl = [A.alloc(8 * 128), A.alloc(8 * 128)]
            wd_cc = [A.alloc(NIC * 128), A.alloc(NIC * 128)]
            gate_sb = A.alloc(512)
            cwbc = A.alloc(512)
            cw_row = A.alloc(512)
            tmp5 = A.alloc(512)
            wr_sb = A.alloc(8 * 16)
            logits = A.alloc(16 * 16)
            cw_cols = A.alloc(16 * 16)

            xb_free = 0
            for tl in range(16):
                wait("sync", "ve", xb_free)
                l = dma_in(xn_b[:, 0:DM], x_in[tl * 128:(tl + 1) * 128, :])
                for grp in range(2):
                    bk = grp
                    for c4 in range(4):
                        c = grp * 4 + c4
                        transp(bk, xn_b[:, c * 128:(c + 1) * 128],
                               ps[bk][:, c4 * 128:(c4 + 1) * 128],
                               wait_list=[("ld", l)] if c4 == 0 else (),
                               bump=(c4 == 3), bank_wait=(c4 == 0))
                    pe_t = cnt["pe"]
                    wait("vector", "pe", pe_t)
                    for c4 in range(4):
                        c = grp * 4 + c4
                        emit("vector", lambda E, c=c, c4=c4, bk=bk, tl=tl:
                             E.tensor_copy(out_acc[:, c * LQ + tl * 128:c * LQ + tl * 128 + 128],
                                           ps[bk][:, c4 * 128:(c4 + 1) * 128]),
                             inc="ve" if c4 == 3 else None)
                    bank_free[bk] = ("ve", cnt["ve"])
                xb_free = cnt["ve"]

            dma_in(wr_sb.rearrange("p (c e) -> p c e", c=8, e=16)[:, :, 0:NE],
                   wr_in.rearrange("(c p) e -> p c e", p=128))
            wr_ld = cnt["ld"]
            for tb in range(4):
                wait("sync", "pe", cnt["pe"])
                l = dma_in(xn_b.rearrange("p (c t) -> p c t", c=8),
                           xn_s.rearrange("(c p) t -> p c t", p=128)[:, :, tb * 512:(tb + 1) * 512])
                for tc4 in range(4):
                    tc = tb * 4 + tc4
                    bk = 2 + tc4 % 2
                    pairs = [(xn_b[:, c * 512 + tc4 * 128:c * 512 + tc4 * 128 + 128],
                              wr_sb[:, c * 16:c * 16 + NE], ps[bk][:, 0:NE]) for c in range(8)]
                    pe_g = mm_group(bk, pairs, wait_list=[("ld", max(l, wr_ld))])
                    wait("vector", "pe", pe_g)
                    emit("vector", lambda E, tc=tc, bk=bk:
                         E.tensor_copy(logits[:, tc * 16:tc * 16 + NE], ps[bk][:, 0:NE]), inc="ve")
                    bank_free[bk] = ("ve", cnt["ve"])
            for tc in range(16):
                lg = logits[:, tc * 16:tc * 16 + NE]
                emit("vector", lambda E, lg=lg: E.reduce_max(S1(M1), lg, axis=AX), inc="ve")
                fence("vector")
                emit("vector", lambda E, lg=lg: E.tensor_scalar(
                    msk9[:, 0:NE], lg, S1(M1), None, OP.is_equal), inc="ve")
                fence("vector")
                emit("vector", lambda E, lg=lg: E.select(
                    tmp5[:, 0:NE], msk9[:, 0:NE], negbig[:, 0:NE], lg), inc="ve")
                fence("vector")
                emit("vector", lambda E: E.reduce_max(S1(M2), tmp5[:, 0:NE], axis=AX), inc="ve")
                fence("vector")
                emit("vector", lambda E: E.tensor_tensor(S1(D12), S1(M1), S1(M2), OP.subtract),
                     inc="ve")
                wait("scalar", "ve", cnt["ve"])
                emit("scalar", lambda E: E.activation(S1(W1), S1(D12), AF.Sigmoid), inc="se")
                wait("vector", "se", cnt["se"])
                emit("vector", lambda E: E.tensor_scalar(S1(W2), S1(W1), -1.0, 1.0, OP.mult, OP.add), inc="ve")
                emit("vector", lambda E, lg=lg: E.tensor_scalar(
                    tmp5[:, 0:NE], lg, S1(M1), S1(W1), OP.is_equal, OP.mult), inc="ve")
                fence("vector")
                emit("vector", lambda E, lg=lg: E.tensor_scalar(
                    tmp5[:, 16:16 + NE], lg, S1(M2), S1(W2), OP.is_equal, OP.mult), inc="ve")
                fence("vector")
                emit("vector", lambda E, tc=tc: E.tensor_tensor(
                    cw_cols[:, tc * 16:tc * 16 + NE], tmp5[:, 0:NE], tmp5[:, 16:16 + NE], OP.add),
                    inc="ve")
            cw_ready = cnt["ve"]

            wgu_free = [("pe", 0), ("pe", 0)]
            wd_free = [("pe", 0), ("pe", 0)]
            gate_free = ("ve", 0)
            hid_free = [("pe", 0)] * NIC
            for e in range(NE):
                for tb in range(LQ // TB3):
                    wait("sync", "pe", cnt["pe"])
                    l_xn = dma_in(xn_b.rearrange("p (c t) -> p c t", c=8),
                                  xn_s.rearrange("(c p) t -> p c t", p=128)
                                  [:, :, tb * TB3:(tb + 1) * TB3])
                    bk = 6
                    for tc4 in range(4):
                        tc = tb * 4 + tc4
                        transp(bk, cw_cols[:, tc * 16 + e:tc * 16 + e + 1],
                               ps[bk][0:1, tc4 * 128:tc4 * 128 + 128],
                               wait_list=[("ve", cw_ready)] if tc4 == 0 else (),
                               bump=(tc4 == 3), bank_wait=(tc4 == 0))
                    wait("vector", "pe", cnt["pe"])
                    emit("vector", lambda E, bk=bk:
                         E.tensor_copy(cw_row[0:1, :], ps[bk][0:1, :]), inc="ve")
                    bank_free[bk] = ("ve", cnt["ve"])
                    bk = 7
                    s, v = bank_free[bk]
                    wait("tensor", s, v)
                    wait("tensor", "ve", cnt["ve"])
                    emit("tensor", lambda E, bk=bk: E.matmul(
                        ps[bk][:, :], ones[0:1, :], cw_row[0:1, :], start=True, stop=True,
                        skip_group_check=True), inc="pe")
                    wait("vector", "pe", cnt["pe"])
                    emit("vector", lambda E, bk=bk: E.tensor_copy(cwbc, ps[bk][:, :]), inc="ve")
                    bank_free[bk] = ("ve", cnt["ve"])
                    cwbc_ready = cnt["ve"]

                    for ic in range(NIC):
                        wb = ic % 2
                        ss, vv = wgu_free[wb]
                        wait("sync", ss, vv)
                        lg_ = dma_in(wg_sl[wb].rearrange("p (c i) -> p c i", c=8),
                                     wg_in[e].rearrange("(c p) i -> p c i", p=128)
                                     [:, :, ic * 128:(ic + 1) * 128])
                        lu_ = dma_in(wu_sl[wb].rearrange("p (c i) -> p c i", c=8),
                                     wu_in[e].rearrange("(c p) i -> p c i", p=128)
                                     [:, :, ic * 128:(ic + 1) * 128])
                        bkg, bku = 0, 1
                        pairs = [(wg_sl[wb][:, c * 128:(c + 1) * 128],
                                  xn_b[:, c * 512:(c + 1) * 512], ps[bkg][:, :]) for c in range(8)]
                        pe_gate = mm_group(bkg, pairs, wait_list=[("ld", lg_), ("ld", l_xn)])
                        ss, vv = gate_free
                        wait("scalar", ss, vv)
                        wait("scalar", "pe", pe_gate)
                        emit("scalar", lambda E, bkg=bkg:
                             E.activation(gate_sb, ps[bkg][:, :], AF.Silu), inc="se")
                        se_gate = cnt["se"]
                        bank_free[bkg] = ("se", se_gate)
                        pairs = [(wu_sl[wb][:, c * 128:(c + 1) * 128],
                                  xn_b[:, c * 512:(c + 1) * 512], ps[bku][:, :]) for c in range(8)]
                        pe_up = mm_group(bku, pairs, wait_list=[("ld", lu_)])
                        wgu_free[wb] = ("pe", cnt["pe"])
                        ss, vv = hid_free[ic]
                        wait("vector", ss, vv)
                        wait("vector", "pe", pe_up)
                        wait("vector", "se", se_gate)
                        emit("vector", lambda E, ic=ic, bku=bku: E.tensor_tensor(
                            hidden[:, ic * TB3:(ic + 1) * TB3], gate_sb, ps[bku][:, :], OP.mult),
                            inc="ve")
                        gate_free = ("ve", cnt["ve"])
                        bank_free[bku] = ("ve", cnt["ve"])
                    hid_ready = cnt["ve"]

                    for hc in range(8):
                        wb = hc % 2
                        ss, vv = wd_free[wb]
                        wait("sync", ss, vv)
                        l_wd = dma_in(wd_cc[wb].rearrange("p (c h) -> p c h", c=NIC),
                                      wd_in[e].rearrange("(c p) h -> p c h", p=128)
                                      [:, :, hc * 128:(hc + 1) * 128])
                        bk = 2 + hc % 2
                        pairs = [(wd_cc[wb][:, j * 128:(j + 1) * 128],
                                  hidden[:, j * TB3:(j + 1) * TB3], ps[bk][:, :])
                                 for j in range(NIC)]
                        pe_d = mm_group(bk, pairs,
                                        wait_list=[("ld", l_wd), ("ve", hid_ready)])
                        wd_free[wb] = ("pe", cnt["pe"])
                        wait("vector", "pe", pe_d)
                        wait("vector", "ve", cwbc_ready)
                        fence("vector")
                        emit("vector", lambda E, bk=bk: E.tensor_tensor(
                            tmp5, ps[bk][:, :], cwbc, OP.mult), inc="ve")
                        fence("vector")
                        oa = out_acc[:, hc * LQ + tb * TB3:hc * LQ + tb * TB3 + TB3]
                        emit("vector", lambda E, oa=oa: E.tensor_tensor(
                            oa, oa, tmp5, OP.add), inc="ve")
                        bank_free[bk] = ("ve", cnt["ve"])
                    hid_free = [("pe", cnt["pe"])] * NIC
            barrier()
            A.release()

        if "2" in phases:
            A.mark()
            K_sb = A.alloc(2 * L)
            V_sb = A.alloc(32 * 256)
            q_sb = A.alloc(2 * LQ)
            S_row = A.alloc(4096)
            P_row = A.alloc(4096)
            PT_sb = [A.alloc(512), A.alloc(512)]
            O_sb = A.alloc(256)

            msk_free = [("ve", 0), ("ve", 0)]
            pt_free = [("pe", 0), ("pe", 0)]
            srow_free = ("se", 0)
            prow_free = ("pe", 0)
            osb_free = ("pe", 0)

            for g in range(G):
                wait("sync", "ve", cnt["ve"])
                wait("sync", "pe", cnt["pe"])
                lK = dma_in(K_sb.rearrange("p (c t) -> p c t", c=2),
                            k_s.rearrange("(c p) t -> p c t", p=128)
                            [:, 2 * g:2 * g + 2, :])
                lV = dma_in(V_sb.rearrange("p (c d) -> p c d", c=32),
                            v_s.rearrange("(c p) d -> p c d", p=128)
                            [:, :, g * 256:(g + 1) * 256])
                lq = dma_in(q_sb.rearrange("p (c t) -> p c t", c=2),
                            q_s.rearrange("(c p) t -> p c t", p=128)
                            [:, 2 * g:2 * g + 2, :])
                for qt in range(16):
                    l0 = qt * 128
                    kt0 = l0 // 512
                    kts = list(range(kt0, 8))
                    wait("vector", "ge", cnt["ge"])
                    emit("vector", lambda E, qt=qt, g=g: E.tensor_scalar(
                        S1(THRQ), dxcols[:, qt:qt + 1], WIN[g], None, OP.add), inc="ve")
                    fence("vector")
                    emit("vector", lambda E, qt=qt: E.tensor_tensor(
                        S1(THRQ), S1(THRQ), qcap[:, qt:qt + 1], OP.min), inc="ve")
                    fence("vector")
                    for ki, kt in enumerate(kts):
                        emit("vector", lambda E, ki=ki, off=float(l0 - kt * 512): E.tensor_scalar(
                            S1(THRK + 1 + ki), S1(THRQ), off, None, OP.add),
                            inc="ve" if ki == len(kts) - 1 else None)
                    fence("vector")
                    thrk_ve = cnt["ve"]
                    for ki, kt in enumerate(kts):
                        m0 = kt * 512
                        var = 1 + (l0 - m0) // 128 if kt == kt0 else 0
                        mb = ki % 2
                        ss, vv = msk_free[mb]
                        wait("gpsimd", ss, vv)
                        if ki == 0:
                            wait("gpsimd", "ve", thrk_ve)
                        emit("gpsimd", lambda E, var=var, mb=mb, ki=ki: E.tensor_scalar(
                            msku[mb][:, :], dmats[:, var * 512:(var + 1) * 512],
                            S1(THRK + 1 + ki), None, OP.is_le), inc="ge")
                        msk_ge = cnt["ge"]
                        bk = ki % 2
                        pairs = [(q_sb[:, d * LQ + l0:d * LQ + l0 + 128],
                                  K_sb[:, d * L + m0:d * L + m0 + 512], ps[bk][:, :])
                                 for d in range(2)]
                        pe_s = mm_group(bk, pairs,
                                        wait_list=[("ld", lq), ("ld", lK)] if ki == 0 and qt == 0 else ())
                        wait("vector", "pe", pe_s)
                        wait("vector", "ge", msk_ge)
                        if ki == 0:
                            ss, vv = srow_free
                            wait("vector", ss, vv)
                        emit("vector", lambda E, kt=kt, mb=mb, bk=bk: E.select(
                            S_row[:, kt * 512:(kt + 1) * 512], msku[mb][:, :],
                            ps[bk][:, :], negbig[:, :]), inc="ve")
                        msk_free[mb] = ("ve", cnt["ve"])
                        bank_free[bk] = ("ve", cnt["ve"])
                    sel_done = cnt["ve"]
                    if debug and g == 0 and qt == 0:
                        dma_out(dbg2, S_row, waits=[("ve", sel_done)])
                    used0 = kt0 * 512
                    ss, vv = srow_free
                    wait("vector", ss, vv)
                    fence("vector")
                    emit("vector", lambda E, used0=used0:
                         E.reduce_max(S1(MAXV), S_row[:, used0:4096], axis=AX), inc="ve")
                    fence("vector")
                    emit("vector", lambda E: E.tensor_scalar(
                        S1(EBIAS), S1(MAXV), -ESC, None, OP.mult), inc="ve")
                    ve_bias = cnt["ve"]
                    ss, vv = prow_free
                    wait("scalar", ss, vv)
                    wait("scalar", "ve", ve_bias)
                    emit("scalar", lambda E, used0=used0: E.activation(
                        P_row[:, used0:4096], S_row[:, used0:4096], AF.Exp,
                        bias=S1(EBIAS), scale=ESC, accum_out=S1(SUMV)), inc="se")
                    se_exp = cnt["se"]
                    srow_free = ("se", se_exp)
                    if debug and g == 0 and qt == 0:
                        dma_out(dbg3, P_row, waits=[("se", se_exp)])
                        dma_out(dbg[:, 16:48], sm[:, 0:32], waits=[("se", se_exp)])
                    wait("vector", "se", se_exp)
                    fence("vector")
                    emit("vector", lambda E: E.reciprocal(S1(RECIP), S1(SUMV)), inc="ve")
                    ve_recip = cnt["ve"]
                    ss, vv = osb_free
                    first_pv = True
                    for ki, kt in enumerate(kts):
                        tb_ = 2 + ki % 2
                        for c4 in range(4):
                            kc = kt * 4 + c4
                            transp(tb_, P_row[:, kc * 128:(kc + 1) * 128],
                                   ps[tb_][:, c4 * 128:(c4 + 1) * 128],
                                   wait_list=[("se", se_exp)] if c4 == 0 else (),
                                   bump=(c4 == 3), bank_wait=(c4 == 0))
                        pe_t = cnt["pe"]
                        sbw = ki % 2
                        sss, vvv = pt_free[sbw]
                        wait("scalar", "pe", max(pe_t, vvv if sss == "pe" else 0))
                        if sss != "pe":
                            wait("scalar", sss, vvv)
                        emit("scalar", lambda E, tb_=tb_, sbw=sbw:
                             E.copy(PT_sb[sbw], ps[tb_][:, :]), inc="se")
                        se_pt = cnt["se"]
                        bank_free[tb_] = ("se", se_pt)
                        ob = 4 + qt % 2
                        if first_pv:
                            s0, v0 = bank_free[ob]
                            wait("tensor", s0, v0)
                        wait("tensor", "se", se_pt)
                        for c4 in range(4):
                            kc = kt * 4 + c4
                            st_ = first_pv and c4 == 0
                            sp_ = (ki == len(kts) - 1) and c4 == 3
                            emit("tensor", lambda E, c4=c4, kc=kc, ob=ob, st_=st_, sp_=sp_, sbw=sbw:
                                 E.matmul(ps[ob][:, 0:256], PT_sb[sbw][:, c4 * 128:(c4 + 1) * 128],
                                          V_sb[:, kc * 256:(kc + 1) * 256],
                                          start=st_, stop=sp_, skip_group_check=True),
                                 inc="pe" if sp_ else None)
                        first_pv = False
                        pt_free[sbw] = ("pe", cnt["pe"])
                    prow_free = ("pe", cnt["pe"])
                    pe_pv = cnt["pe"]
                    ob = 4 + qt % 2
                    ss, vv = osb_free
                    wait("vector", ss, vv)
                    wait("vector", "pe", pe_pv)
                    wait("vector", "ve", ve_recip)
                    emit("vector", lambda E, ob=ob: E.tensor_scalar(
                        O_sb, ps[ob][:, 0:256], S1(RECIP), None, OP.mult), inc="ve")
                    ve_o = cnt["ve"]
                    bank_free[ob] = ("ve", ve_o)
                    bk = 6
                    for c2 in range(2):
                        transp(bk, O_sb[:, c2 * 128:(c2 + 1) * 128],
                               ps[bk][:, c2 * 128:(c2 + 1) * 128],
                               wait_list=[("ve", ve_o)] if c2 == 0 else (),
                               bump=(c2 == 1), bank_wait=(c2 == 0))
                    osb_free = ("pe", cnt["pe"])
                    wait("vector", "pe", cnt["pe"])
                    for c2 in range(2):
                        fc = 2 * g + c2
                        oa = out_acc[:, fc * LQ + l0:fc * LQ + l0 + 128]
                        emit("vector", lambda E, oa=oa, c2=c2, bk=bk: E.tensor_tensor(
                            oa, oa, ps[bk][:, c2 * 128:(c2 + 1) * 128], OP.add),
                            inc="ve" if c2 == 1 else None)
                    bank_free[bk] = ("ve", cnt["ve"])
            barrier()
            A.release()

        A.mark()
        stage = [A.alloc(DM), A.alloc(DM)]
        stage_free = [("st", 0), ("st", 0)]
        for tl in range(16):
            sb = tl % 2
            for grp in range(2):
                bk = grp
                for c4 in range(4):
                    c = grp * 4 + c4
                    transp(bk, out_acc[:, c * LQ + tl * 128:c * LQ + tl * 128 + 128],
                           ps[bk][:, c4 * 128:(c4 + 1) * 128],
                           bump=(c4 == 3), bank_wait=(c4 == 0))
                pe_t = cnt["pe"]
                ss, vv = stage_free[sb]
                wait("vector", ss, vv)
                wait("vector", "pe", pe_t)
                for c4 in range(4):
                    c = grp * 4 + c4
                    emit("vector", lambda E, c=c, c4=c4, bk=bk, sb=sb:
                         E.tensor_copy(stage[sb][:, c * 128:(c + 1) * 128],
                                       ps[bk][:, c4 * 128:(c4 + 1) * 128]),
                         inc="ve" if c4 == 3 else None)
                bank_free[bk] = ("ve", cnt["ve"])
            st_c = dma_out(out_d[tl * 128:(tl + 1) * 128, :], stage[sb],
                           waits=[("ve", cnt["ve"])])
            stage_free[sb] = ("st", st_c)
        if debug:
            emit("gpsimd", lambda E: E.dma_start(out=dbg[:, 0:16], in_=dxcols),
                 inc="st", incn=16)
        A.release()
        A.release()

        wait("gpsimd", "st", cnt["st"])
        wait("gpsimd", "ld", cnt["ld"])

        with nc.Block() as block:
            @block.sync
            def _(E):
                for f in ops["sync"]:
                    f(E)

            @block.tensor
            def _(E):
                for f in ops["tensor"]:
                    f(E)

            @block.vector
            def _(E):
                for f in ops["vector"]:
                    f(E)

            @block.scalar
            def _(E):
                for f in ops["scalar"]:
                    f(E)

            @block.gpsimd
            def _(E):
                for f in ops["gpsimd"]:
                    f(E)

    return nc


def make_inputs(x_b, h, Wq, Wqkv, bqkv, Wop, W2, b2, Wr, Wg, Wu, Wd):
    xc = np.ascontiguousarray(np.roll(x_b, -2048 * h, axis=0))
    J0 = 4095 if h == 0 else 2047
    p = np.arange(128)[:, None]
    i = np.arange(16)[None, :]
    qcap = (J0 - (i * 128 + p)).astype(np.float32)
    dmats = np.empty((128, 5 * 512), np.float32)
    j = np.arange(512)[None, :]
    base = (j - p).astype(np.float32)
    dmats[:, 0:512] = base
    for v in range(4):
        r = v * 128
        m = np.where((j - p) < r, np.float32(1e9), base)
        dmats[:, (v + 1) * 512:(v + 2) * 512] = m
    return {
        "x": xc,
        "qcap": qcap,
        "wqt": np.ascontiguousarray(Wq.T),
        "wqkvt": np.ascontiguousarray(Wqkv.T),
        "bqkv_kc": np.ascontiguousarray(bqkv[1024:2048].reshape(8, 128).T),
        "bqkv_qc": np.ascontiguousarray(bqkv[0:1024].reshape(8, 128).T),
        "bqkv_vb": np.ascontiguousarray(np.broadcast_to(bqkv[2048:3072][None, :], (128, 1024))),
        "wopt": np.ascontiguousarray(Wop.T),
        "w2t": np.ascontiguousarray(W2.reshape(2, 128).T),
        "b2c": np.ascontiguousarray(np.broadcast_to(b2.reshape(1, 1), (128, 1))),
        "ident": np.eye(128, dtype=np.float32),
        "dmats": dmats,
        "wr": Wr,
        "wg": Wg,
        "wu": Wu,
        "wd": Wd,
    }


def kernel(L0_attn_x, L0_attn_Wq, L0_attn_Wk, L0_attn_Wv, L0_attn_Wqkv, L0_attn_bqkv,
           L0_attn_Wop, L0_attn_W2, L0_attn_b2, L0_ffn_Wr, L0_ffn_Wg, L0_ffn_Wu,
           L0_ffn_Wd):
    x = np.ascontiguousarray(np.asarray(L0_attn_x, dtype=np.float32))
    args = [np.ascontiguousarray(np.asarray(a, dtype=np.float32)) for a in
            (L0_attn_Wq, L0_attn_Wqkv, L0_attn_bqkv, L0_attn_Wop, L0_attn_W2,
             L0_attn_b2, L0_ffn_Wr, L0_ffn_Wg, L0_ffn_Wu, L0_ffn_Wd)]
    nc = build()
    in_maps = [make_inputs(x[c // 2], c % 2, *args) for c in range(8)]
    res = run_bass_kernel_spmd(nc, in_maps, core_ids=list(range(8)))
    out = np.empty((4, 4096, 1024), np.float32)
    for c in range(8):
        b, h = c // 2, c % 2
        out[b, 2048 * h:2048 * (h + 1), :] = np.asarray(res.results[c]["out"])
    return out


# version 02
# speedup vs baseline: 1.0149x; 1.0149x over previous
import numpy as np
import concourse.bass as bass
import concourse.mybir as mybir
from concourse.bass_utils import run_bass_kernel_spmd
from concourse.alu_op_type import AluOpType as OP

F32 = mybir.dt.float32
U8 = mybir.dt.uint8
AF = mybir.ActivationFunctionType
AX = mybir.AxisListType.X

L, DM, LQ = 4096, 1024, 2048
G, DH = 4, 256
WIN = (32.0, 64.0, 128.0, 256.0)
ESC = 2.0 / 64.0
NE, DI = 9, 2816
NIC = DI // 128
NEG = -1.0e18
EPS = 1e-5
TB3 = 512
ENGS = ("sync", "tensor", "vector", "scalar", "gpsimd")
ARENA_W = 52000


def build(phases=("1a", "1b", "3", "2"), debug=False):
    nc = bass.Bass("TRN2", target_bir_lowering=False)

    x_in = nc.dram_tensor("x", [L, DM], F32, kind="ExternalInput").ap()
    qcap_in = nc.dram_tensor("qcap", [128, 16], F32, kind="ExternalInput").ap()
    wqt_in = nc.dram_tensor("wqt", [DM, DM], F32, kind="ExternalInput").ap()
    wqkvt_in = nc.dram_tensor("wqkvt", [DM, 3 * DM], F32, kind="ExternalInput").ap()
    bqkv_kc_in = nc.dram_tensor("bqkv_kc", [128, 8], F32, kind="ExternalInput").ap()
    bqkv_qc_in = nc.dram_tensor("bqkv_qc", [128, 8], F32, kind="ExternalInput").ap()
    bqkv_vb_in = nc.dram_tensor("bqkv_vb", [128, DM], F32, kind="ExternalInput").ap()
    wopt_in = nc.dram_tensor("wopt", [DM, 256], F32, kind="ExternalInput").ap()
    w2t_in = nc.dram_tensor("w2t", [128, 2], F32, kind="ExternalInput").ap()
    b2c_in = nc.dram_tensor("b2c", [128, 1], F32, kind="ExternalInput").ap()
    ident_in = nc.dram_tensor("ident", [128, 128], F32, kind="ExternalInput").ap()
    dmats_in = nc.dram_tensor("dmats", [128, 5 * 512], F32, kind="ExternalInput").ap()
    wr_in = nc.dram_tensor("wr", [DM, NE], F32, kind="ExternalInput").ap()
    wg_in = nc.dram_tensor("wg", [NE, DM, DI], F32, kind="ExternalInput").ap()
    wu_in = nc.dram_tensor("wu", [NE, DM, DI], F32, kind="ExternalInput").ap()
    wd_in = nc.dram_tensor("wd", [NE, DI, DM], F32, kind="ExternalInput").ap()
    out_d = nc.dram_tensor("out", [LQ, DM], F32, kind="ExternalOutput").ap()

    skind = dict(kind="ExternalOutput") if debug else {}
    k_s = nc.dram_tensor("k_s", [DM, L], F32, **skind).ap()
    v_s = nc.dram_tensor("v_s", [L, DM], F32, **skind).ap()
    q_s = nc.dram_tensor("q_s", [DM, LQ], F32, **skind).ap()
    qin_s = nc.dram_tensor("qin_s", [DM, LQ], F32, **skind).ap()
    xn_s = nc.dram_tensor("xn_s", [DM, LQ], F32, **skind).ap()
    dbg = nc.dram_tensor("dbg", [128, 512], F32, kind="ExternalOutput").ap() if debug else None
    dbg2 = nc.dram_tensor("dbg2", [128, 4096], F32, kind="ExternalOutput").ap() if debug else None
    dbg3 = nc.dram_tensor("dbg3", [128, 4096], F32, kind="ExternalOutput").ap() if debug else None

    import contextlib
    es = contextlib.ExitStack()
    with es:
        arena_t = es.enter_context(nc.sbuf_tensor("arena", [128, ARENA_W], F32))
        msku = [es.enter_context(nc.sbuf_tensor(f"msku{i}", [128, 512], U8)) for i in range(2)]
        msk9 = es.enter_context(nc.sbuf_tensor("msk9", [128, 16], U8))
        ps = [es.enter_context(nc.psum_tensor(f"ps{i}", [128, 512], F32)) for i in range(8)]
        NLD, NST = 16, 8
        sem_names = ([f"ld{i}" for i in range(NLD)] + [f"st{i}" for i in range(NST)]
                     + ["pe", "ve", "se", "ge"])
        sem = {n: es.enter_context(nc.semaphore(f"s_{n}")) for n in sem_names}

        ops = {e: [] for e in ENGS}
        cnt = {n: 0 for n in sem}

        def on(e, f):
            ops[e].append(f)

        def wait(e, s, v):
            if v > 0:
                on(e, lambda E, s=s, v=v: E.wait_ge(sem[s], v))

        def emit(e, builder, inc=None, incn=1):
            if inc is None:
                on(e, builder)
                return None
            cnt[inc] += incn
            on(e, lambda E, b=builder, s=inc, n=incn: b(E).then_inc(sem[s], n))
            return cnt[inc]

        OWN = {"vector": "ve", "scalar": "se", "gpsimd": "ge", "tensor": "pe"}

        def fence(e):
            wait(e, OWN[e], cnt[OWN[e]])

        dma_idx = [0, 0]

        def dma_in(dst, src):
            name = f"ld{dma_idx[0] % NLD}"
            dma_idx[0] += 1
            emit("sync", lambda E, d=dst, s=src: E.dma_start(out=d, in_=s),
                 inc=name, incn=16)
            return (name, cnt[name])

        def dma_out(dst, src, waits=()):
            for s, v in waits:
                wait("gpsimd", s, v)
            name = f"st{dma_idx[1] % NST}"
            dma_idx[1] += 1
            emit("gpsimd", lambda E, d=dst, s=src: E.dma_start(out=d, in_=s),
                 inc=name, incn=16)
            return (name, cnt[name])

        def barrier():
            snap = dict(cnt)
            own = {"tensor": ("pe",), "vector": ("ve",), "scalar": ("se",),
                   "gpsimd": ("ge",), "sync": ()}
            for e in ENGS:
                for s, v in snap.items():
                    if s not in own[e]:
                        wait(e, s, v)

        class Arena:
            def __init__(self):
                self.off = 0
                self.stack = []

            def alloc(self, w):
                w16 = (w + 15) // 16 * 16
                assert self.off + w16 <= ARENA_W, f"arena overflow: {self.off + w16}"
                ap = arena_t[:, self.off:self.off + w]
                self.off += w16
                return ap

            def mark(self):
                self.stack.append(self.off)

            def release(self):
                self.off = self.stack.pop()

        A = Arena()

        ident = A.alloc(128)
        dmats = A.alloc(5 * 512)
        negbig = A.alloc(512)
        qcap = A.alloc(16)
        b2c = A.alloc(16)[:, 0:1]
        w2t = A.alloc(16)
        ones = A.alloc(128)
        dxcols = A.alloc(16)
        sm = A.alloc(64)

        def S1(i):
            return sm[:, i:i + 1]

        XSUM, SUMSQ, MEAN, VAR, SD, RSIG, NB = 0, 1, 2, 3, 4, 5, 6
        MAXV, EBIAS, SUMV, RECIP, THRQ, THRK = 7, 8, 9, 10, 11, 12
        M1, M2, D12, W1, W2, SGT = 13, 14, 15, 16, 17, 18

        consts_toks = [dma_in(ident, ident_in), dma_in(dmats, dmats_in),
                       dma_in(qcap, qcap_in), dma_in(b2c, b2c_in),
                       dma_in(w2t[:, 0:2], w2t_in[:, 0:2])]
        emit("gpsimd", lambda E: E.memset(negbig[:, :], NEG), inc="ge")
        emit("gpsimd", lambda E: E.memset(ones[0:1, :], 1.0), inc="ge")

        bank_free = [("ve", 0)] * 8

        def mm_group(bank, pairs, wait_list=(), bump=True):
            s, v = bank_free[bank]
            wait("tensor", s, v)
            for ss, vv in wait_list:
                wait("tensor", ss, vv)
            n = len(pairs)
            for i, (lh, rh, outap) in enumerate(pairs):
                inc = "pe" if (bump and i == n - 1) else None
                emit("tensor",
                     lambda E, o=outap, l=lh, r=rh, st=(i == 0), sp=(i == n - 1):
                     E.matmul(o, l, r, start=st, stop=sp, skip_group_check=True),
                     inc=inc)
            return cnt["pe"]

        def transp(bank, in_ap, out_ap, wait_list=(), bump=True, bank_wait=True):
            if bank_wait:
                s, v = bank_free[bank]
                wait("tensor", s, v)
            for ss, vv in wait_list:
                wait("tensor", ss, vv)
            return emit("tensor",
                        lambda E, o=out_ap, i=in_ap: E.transpose(o, i, ident),
                        inc="pe" if bump else None)

        if "1a" in phases:
            A.mark()
            wqt = A.alloc(8 * DM)
            wqkvt = A.alloc(8 * 3 * DM)
            bqkv_kc = A.alloc(16)
            bqkv_qc = A.alloc(16)
            bqkv_vb = A.alloc(DM)
            x_t = [A.alloc(DM), A.alloc(DM)]
            xn_t = [A.alloc(DM), A.alloc(DM)]
            xn_fm = A.alloc(8 * 512)
            qin_fm = A.alloc(8 * 512)
            stg = [A.alloc(512), A.alloc(512)]

            w_toks = [dma_in(wqt.rearrange("p (c o) -> p c o", c=8),
                             wqt_in.rearrange("(c p) o -> p c o", p=128)),
                      dma_in(wqkvt.rearrange("p (c o) -> p c o", c=8),
                             wqkvt_in.rearrange("(c p) o -> p c o", p=128)),
                      dma_in(bqkv_kc[:, 0:8], bqkv_kc_in),
                      dma_in(bqkv_qc[:, 0:8], bqkv_qc_in),
                      dma_in(bqkv_vb, bqkv_vb_in)]
            for t_ in consts_toks:
                wait("tensor", *t_)

            xbuf_free = [("se", 0), ("se", 0)]
            xnfm_dma_st = ("st0", 0)
            qinfm_dma_st = ("st0", 0)
            xnbuf_free = [("pe", 0), ("pe", 0)]
            stg_free = [("st0", 0), ("st0", 0)]
            stats_read = 0

            for tb in range(8):
                own = tb < 4
                for s4 in range(4):
                    t0 = tb * 512 + s4 * 128
                    bi = s4 % 2
                    xe, xne = x_t[bi], xn_t[bi]
                    ss, vv = xbuf_free[bi]
                    wait("sync", ss, vv)
                    l = dma_in(xe, x_in[t0:t0 + 128, :])
                    ss, vv = xnbuf_free[bi]
                    wait("scalar", ss, vv)
                    wait("scalar", *l)
                    wait("scalar", "ve", stats_read)
                    emit("scalar", lambda E, xn=xne, x=xe:
                         E.activation(xn, x, AF.Square, accum_out=S1(SUMSQ)), inc="se")
                    se_sq = cnt["se"]
                    wait("vector", *l)
                    emit("vector", lambda E, x=xe: E.reduce_sum(S1(XSUM), x, axis=AX), inc="ve")
                    fence("vector")
                    wait("vector", "se", se_sq)
                    emit("vector", lambda E: E.tensor_scalar(S1(MEAN), S1(XSUM), 1.0 / DM, None, OP.mult), inc="ve")
                    emit("vector", lambda E: E.tensor_scalar(S1(VAR), S1(SUMSQ), 1.0 / DM, None, OP.mult), inc="ve")
                    fence("vector")
                    emit("vector", lambda E: E.tensor_tensor(S1(SGT), S1(MEAN), S1(MEAN), OP.mult), inc="ve")
                    fence("vector")
                    emit("vector", lambda E: E.scalar_tensor_tensor(
                        S1(SGT), S1(SGT), -1.0, S1(VAR), OP.mult, OP.add), inc="ve")
                    fence("vector")
                    emit("vector", lambda E: E.tensor_scalar(S1(VAR), S1(SGT), EPS, None, OP.add), inc="ve")
                    ve_var = cnt["ve"]
                    wait("scalar", "ve", ve_var)
                    emit("scalar", lambda E: E.activation(S1(SD), S1(VAR), AF.Sqrt), inc="se")
                    se_sd = cnt["se"]
                    wait("vector", "se", se_sd)
                    emit("vector", lambda E: E.reciprocal(S1(RSIG), S1(SD)), inc="ve")
                    fence("vector")
                    emit("vector", lambda E: E.scalar_tensor_tensor(
                        S1(NB), S1(MEAN), -1.0, S1(RSIG), OP.mult, OP.mult), inc="ve")
                    fence("vector")
                    stats_read = cnt["ve"]
                    ss, vv = xnbuf_free[bi]
                    wait("vector", ss, vv)
                    emit("vector", lambda E, xn=xne, x=xe: E.tensor_scalar(
                        xn, x, S1(RSIG), S1(NB), OP.mult, OP.add), inc="ve")
                    ve_xn = cnt["ve"]
                    xbuf_free[bi] = ("ve", ve_xn)
                    if s4 == 0:
                        wait("vector", *xnfm_dma_st)
                    for grp in range(2):
                        bk = grp
                        wl = [("ve", ve_xn)]
                        for c4 in range(4):
                            c = grp * 4 + c4
                            transp(bk, xne[:, c * 128:(c + 1) * 128],
                                   ps[bk][:, c4 * 128:(c4 + 1) * 128],
                                   wait_list=wl if c4 == 0 else (),
                                   bump=(c4 == 3), bank_wait=(c4 == 0))
                        pe_t = cnt["pe"]
                        wait("vector", "pe", pe_t)
                        for c4 in range(4):
                            c = grp * 4 + c4
                            emit("vector", lambda E, c=c, c4=c4, bk=bk, s4=s4:
                                 E.tensor_copy(xn_fm[:, c * 512 + s4 * 128:c * 512 + s4 * 128 + 128],
                                               ps[bk][:, c4 * 128:(c4 + 1) * 128]),
                                 inc="ve" if c4 == 3 else None)
                        bank_free[bk] = ("ve", cnt["ve"])
                    xnbuf_free[bi] = ("pe", cnt["pe"])
                xnfm_ready = cnt["ve"]

                if own:
                    xnfm_dma_st = dma_out(
                        xn_s.rearrange("(c p) t -> p c t", p=128)[:, :, tb * 512:(tb + 1) * 512],
                        xn_fm.rearrange("p (c t) -> p c t", c=8),
                        waits=[("ve", xnfm_ready)])

                wait("vector", *qinfm_dma_st)
                for oc in range(8):
                    bk = 2 + oc % 2
                    pairs = [(wqt[:, c * DM + oc * 128:c * DM + oc * 128 + 128],
                              xn_fm[:, c * 512:(c + 1) * 512], ps[bk][:, :]) for c in range(8)]
                    pe_g = mm_group(bk, pairs,
                                    wait_list=[("ve", xnfm_ready)] + w_toks)
                    wait("vector", "pe", pe_g)
                    emit("vector", lambda E, oc=oc, bk=bk:
                         E.tensor_copy(qin_fm[:, oc * 512:(oc + 1) * 512], ps[bk][:, :]), inc="ve")
                    bank_free[bk] = ("ve", cnt["ve"])
                qinfm_ready = cnt["ve"]
                if own:
                    qinfm_dma_st = dma_out(
                        qin_s.rearrange("(c p) t -> p c t", p=128)[:, :, tb * 512:(tb + 1) * 512],
                        qin_fm.rearrange("p (c t) -> p c t", c=8),
                        waits=[("ve", qinfm_ready)])

                dsts = [("k", 1024, bqkv_kc, k_s, L, tb * 512)]
                if own:
                    dsts.append(("q", 0, bqkv_qc, q_s, LQ, tb * 512))
                for name, obase, bias_c, dstt, dw, tcol in dsts:
                    for oc in range(8):
                        bk = 2 + oc % 2
                        pairs = [(wqkvt[:, c * 3 * DM + obase + oc * 128:
                                        c * 3 * DM + obase + oc * 128 + 128],
                                  qin_fm[:, c * 512:(c + 1) * 512], ps[bk][:, :])
                                 for c in range(8)]
                        pe_g = mm_group(bk, pairs,
                                        wait_list=[("ve", qinfm_ready)] + w_toks)
                        sb = oc % 2
                        wait("vector", *stg_free[sb])
                        for t_ in w_toks:
                            wait("vector", *t_)
                        wait("vector", "pe", pe_g)
                        emit("vector", lambda E, bk=bk, sb=sb, bias_c=bias_c, oc=oc:
                             E.tensor_scalar(stg[sb], ps[bk][:, :],
                                             bias_c[:, oc:oc + 1], None, OP.add), inc="ve")
                        bank_free[bk] = ("ve", cnt["ve"])
                        stg_free[sb] = dma_out(dstt[oc * 128:(oc + 1) * 128, tcol:tcol + 512],
                                               stg[sb], waits=[("ve", cnt["ve"])])

                for s4 in range(4):
                    for half in range(2):
                        bk = 4 + half
                        pairs = [(qin_fm[:, c * 512 + s4 * 128:c * 512 + s4 * 128 + 128],
                                  wqkvt[:, c * 3 * DM + 2048 + half * 512:
                                        c * 3 * DM + 2048 + half * 512 + 512], ps[bk][:, :])
                                 for c in range(8)]
                        pe_g = mm_group(bk, pairs,
                                        wait_list=[("ve", qinfm_ready)] + w_toks)
                        sb = half
                        wait("vector", *stg_free[sb])
                        for t_ in w_toks:
                            wait("vector", *t_)
                        wait("vector", "pe", pe_g)
                        emit("vector", lambda E, bk=bk, sb=sb, half=half:
                             E.scalar_tensor_tensor(stg[sb], ps[bk][:, :], 1.0,
                                                    bqkv_vb[:, half * 512:half * 512 + 512],
                                                    OP.mult, OP.add), inc="ve")
                        bank_free[bk] = ("ve", cnt["ve"])
                        stg_free[sb] = dma_out(v_s[tb * 512 + s4 * 128:tb * 512 + s4 * 128 + 128,
                                                   half * 512:half * 512 + 512], stg[sb],
                                               waits=[("ve", cnt["ve"])])
            barrier()
            A.release()

        if "1b" in phases:
            A.mark()
            wopt = A.alloc(8 * 256)
            qin_b = A.alloc(8 * 512)
            relu_b = A.alloc(8 * 512)
            h_fm = A.alloc(2 * LQ)
            wop_tok = dma_in(wopt.rearrange("p (c o) -> p c o", c=8),
                             wopt_in.rearrange("(c p) o -> p c o", p=128))
            relu_read = 0
            for blk in range(4):
                wait("sync", "se", relu_read)
                l = dma_in(qin_b.rearrange("p (c t) -> p c t", c=8),
                           qin_s.rearrange("(c p) t -> p c t", p=128)[:, :, blk * 512:(blk + 1) * 512])
                wait("scalar", *l)
                wait("scalar", "pe", cnt["pe"])
                emit("scalar", lambda E: E.activation(relu_b, qin_b, AF.Relu), inc="se")
                relu_read = cnt["se"]
                for oc in range(2):
                    bk = oc
                    pairs = [(wopt[:, c * 256 + oc * 128:c * 256 + oc * 128 + 128],
                              relu_b[:, c * 512:(c + 1) * 512], ps[bk][:, :]) for c in range(8)]
                    pe_g = mm_group(bk, pairs, wait_list=[("se", relu_read), wop_tok])
                    wait("vector", "pe", pe_g)
                    emit("vector", lambda E, oc=oc, blk=blk, bk=bk:
                         E.tensor_copy(h_fm[:, oc * LQ + blk * 512:oc * LQ + blk * 512 + 512],
                                       ps[bk][:, :]), inc="ve")
                    bank_free[bk] = ("ve", cnt["ve"])
            h_ready = cnt["ve"]
            for tc in range(16):
                bk = 2 + tc % 2
                pairs = [(h_fm[:, oc * LQ + tc * 128:oc * LQ + tc * 128 + 128],
                          w2t[:, oc:oc + 1], ps[bk][:, 0:1]) for oc in range(2)]
                pe_g = mm_group(bk, pairs, wait_list=[("ve", h_ready)])
                wait("scalar", "pe", pe_g)
                wait("scalar", "ve", cnt["ve"])
                emit("scalar", lambda E, bk=bk: E.activation(
                    S1(SGT), ps[bk][:, 0:1], AF.Sigmoid, bias=b2c, scale=1.0), inc="se")
                wait("vector", "se", cnt["se"])
                emit("vector", lambda E, tc=tc: E.tensor_scalar(
                    dxcols[:, tc:tc + 1], S1(SGT), float(L), None, OP.mult), inc="ve")
                bank_free[bk] = ("ve", cnt["ve"])
            barrier()
            A.release()

        A.mark()
        out_acc = A.alloc(8 * LQ)

        if "3" in phases:
            A.mark()
            hidden = A.alloc(NIC * TB3)
            xn_b = A.alloc(8 * 512)
            wg_sl = [A.alloc(8 * 128), A.alloc(8 * 128)]
            wu_sl = [A.alloc(8 * 128), A.alloc(8 * 128)]
            wd_cc = [A.alloc(NIC * 128), A.alloc(NIC * 128)]
            gate_sb = A.alloc(512)
            cwbc = A.alloc(512)
            cw_row = A.alloc(512)
            tmp5 = A.alloc(512)
            wr_sb = A.alloc(8 * 16)
            logits = A.alloc(16 * 16)
            cw_cols = A.alloc(16 * 16)

            xb_free = 0
            for tl in range(16):
                wait("sync", "ve", xb_free)
                l = dma_in(xn_b[:, 0:DM], x_in[tl * 128:(tl + 1) * 128, :])
                for grp in range(2):
                    bk = grp
                    for c4 in range(4):
                        c = grp * 4 + c4
                        transp(bk, xn_b[:, c * 128:(c + 1) * 128],
                               ps[bk][:, c4 * 128:(c4 + 1) * 128],
                               wait_list=[l] if c4 == 0 else (),
                               bump=(c4 == 3), bank_wait=(c4 == 0))
                    pe_t = cnt["pe"]
                    wait("vector", "pe", pe_t)
                    for c4 in range(4):
                        c = grp * 4 + c4
                        emit("vector", lambda E, c=c, c4=c4, bk=bk, tl=tl:
                             E.tensor_copy(out_acc[:, c * LQ + tl * 128:c * LQ + tl * 128 + 128],
                                           ps[bk][:, c4 * 128:(c4 + 1) * 128]),
                             inc="ve" if c4 == 3 else None)
                    bank_free[bk] = ("ve", cnt["ve"])
                xb_free = cnt["ve"]

            wr_tok = dma_in(wr_sb.rearrange("p (c e) -> p c e", c=8, e=16)[:, :, 0:NE],
                            wr_in.rearrange("(c p) e -> p c e", p=128))
            for tb in range(4):
                wait("sync", "pe", cnt["pe"])
                l = dma_in(xn_b.rearrange("p (c t) -> p c t", c=8),
                           xn_s.rearrange("(c p) t -> p c t", p=128)[:, :, tb * 512:(tb + 1) * 512])
                for tc4 in range(4):
                    tc = tb * 4 + tc4
                    bk = 2 + tc4 % 2
                    pairs = [(xn_b[:, c * 512 + tc4 * 128:c * 512 + tc4 * 128 + 128],
                              wr_sb[:, c * 16:c * 16 + NE], ps[bk][:, 0:NE]) for c in range(8)]
                    pe_g = mm_group(bk, pairs, wait_list=[l, wr_tok])
                    wait("vector", "pe", pe_g)
                    emit("vector", lambda E, tc=tc, bk=bk:
                         E.tensor_copy(logits[:, tc * 16:tc * 16 + NE], ps[bk][:, 0:NE]), inc="ve")
                    bank_free[bk] = ("ve", cnt["ve"])
            for tc in range(16):
                lg = logits[:, tc * 16:tc * 16 + NE]
                emit("vector", lambda E, lg=lg: E.reduce_max(S1(M1), lg, axis=AX), inc="ve")
                fence("vector")
                emit("vector", lambda E, lg=lg: E.tensor_scalar(
                    msk9[:, 0:NE], lg, S1(M1), None, OP.is_equal), inc="ve")
                fence("vector")
                emit("vector", lambda E, lg=lg: E.select(
                    tmp5[:, 0:NE], msk9[:, 0:NE], negbig[:, 0:NE], lg), inc="ve")
                fence("vector")
                emit("vector", lambda E: E.reduce_max(S1(M2), tmp5[:, 0:NE], axis=AX), inc="ve")
                fence("vector")
                emit("vector", lambda E: E.tensor_tensor(S1(D12), S1(M1), S1(M2), OP.subtract),
                     inc="ve")
                wait("scalar", "ve", cnt["ve"])
                emit("scalar", lambda E: E.activation(S1(W1), S1(D12), AF.Sigmoid), inc="se")
                wait("vector", "se", cnt["se"])
                emit("vector", lambda E: E.tensor_scalar(S1(W2), S1(W1), -1.0, 1.0, OP.mult, OP.add), inc="ve")
                emit("vector", lambda E, lg=lg: E.tensor_scalar(
                    tmp5[:, 0:NE], lg, S1(M1), S1(W1), OP.is_equal, OP.mult), inc="ve")
                fence("vector")
                emit("vector", lambda E, lg=lg: E.tensor_scalar(
                    tmp5[:, 16:16 + NE], lg, S1(M2), S1(W2), OP.is_equal, OP.mult), inc="ve")
                fence("vector")
                emit("vector", lambda E, tc=tc: E.tensor_tensor(
                    cw_cols[:, tc * 16:tc * 16 + NE], tmp5[:, 0:NE], tmp5[:, 16:16 + NE], OP.add),
                    inc="ve")
            cw_ready = cnt["ve"]

            wgu_free = [("pe", 0), ("pe", 0)]
            wd_free = [("pe", 0), ("pe", 0)]
            gate_free = ("ve", 0)
            hid_free = [("pe", 0)] * NIC
            for e in range(NE):
                for tb in range(LQ // TB3):
                    wait("sync", "pe", cnt["pe"])
                    l_xn = dma_in(xn_b.rearrange("p (c t) -> p c t", c=8),
                                  xn_s.rearrange("(c p) t -> p c t", p=128)
                                  [:, :, tb * TB3:(tb + 1) * TB3])
                    bk = 6
                    for tc4 in range(4):
                        tc = tb * 4 + tc4
                        transp(bk, cw_cols[:, tc * 16 + e:tc * 16 + e + 1],
                               ps[bk][0:1, tc4 * 128:tc4 * 128 + 128],
                               wait_list=[("ve", cw_ready)] if tc4 == 0 else (),
                               bump=(tc4 == 3), bank_wait=(tc4 == 0))
                    wait("vector", "pe", cnt["pe"])
                    emit("vector", lambda E, bk=bk:
                         E.tensor_copy(cw_row[0:1, :], ps[bk][0:1, :]), inc="ve")
                    bank_free[bk] = ("ve", cnt["ve"])
                    bk = 7
                    s, v = bank_free[bk]
                    wait("tensor", s, v)
                    wait("tensor", "ve", cnt["ve"])
                    emit("tensor", lambda E, bk=bk: E.matmul(
                        ps[bk][:, :], ones[0:1, :], cw_row[0:1, :], start=True, stop=True,
                        skip_group_check=True), inc="pe")
                    wait("vector", "pe", cnt["pe"])
                    emit("vector", lambda E, bk=bk: E.tensor_copy(cwbc, ps[bk][:, :]), inc="ve")
                    bank_free[bk] = ("ve", cnt["ve"])
                    cwbc_ready = cnt["ve"]

                    for ic in range(NIC):
                        wb = ic % 2
                        ss, vv = wgu_free[wb]
                        wait("sync", ss, vv)
                        lg_ = dma_in(wg_sl[wb].rearrange("p (c i) -> p c i", c=8),
                                     wg_in[e].rearrange("(c p) i -> p c i", p=128)
                                     [:, :, ic * 128:(ic + 1) * 128])
                        lu_ = dma_in(wu_sl[wb].rearrange("p (c i) -> p c i", c=8),
                                     wu_in[e].rearrange("(c p) i -> p c i", p=128)
                                     [:, :, ic * 128:(ic + 1) * 128])
                        bkg, bku = 0, 1
                        pairs = [(wg_sl[wb][:, c * 128:(c + 1) * 128],
                                  xn_b[:, c * 512:(c + 1) * 512], ps[bkg][:, :]) for c in range(8)]
                        pe_gate = mm_group(bkg, pairs, wait_list=[lg_, l_xn])
                        ss, vv = gate_free
                        wait("scalar", ss, vv)
                        wait("scalar", "pe", pe_gate)
                        emit("scalar", lambda E, bkg=bkg:
                             E.activation(gate_sb, ps[bkg][:, :], AF.Silu), inc="se")
                        se_gate = cnt["se"]
                        bank_free[bkg] = ("se", se_gate)
                        pairs = [(wu_sl[wb][:, c * 128:(c + 1) * 128],
                                  xn_b[:, c * 512:(c + 1) * 512], ps[bku][:, :]) for c in range(8)]
                        pe_up = mm_group(bku, pairs, wait_list=[lu_])
                        wgu_free[wb] = ("pe", cnt["pe"])
                        ss, vv = hid_free[ic]
                        wait("vector", ss, vv)
                        wait("vector", "pe", pe_up)
                        wait("vector", "se", se_gate)
                        emit("vector", lambda E, ic=ic, bku=bku: E.tensor_tensor(
                            hidden[:, ic * TB3:(ic + 1) * TB3], gate_sb, ps[bku][:, :], OP.mult),
                            inc="ve")
                        gate_free = ("ve", cnt["ve"])
                        bank_free[bku] = ("ve", cnt["ve"])
                    hid_ready = cnt["ve"]

                    for hc in range(8):
                        wb = hc % 2
                        ss, vv = wd_free[wb]
                        wait("sync", ss, vv)
                        l_wd = dma_in(wd_cc[wb].rearrange("p (c h) -> p c h", c=NIC),
                                      wd_in[e].rearrange("(c p) h -> p c h", p=128)
                                      [:, :, hc * 128:(hc + 1) * 128])
                        bk = 2 + hc % 2
                        pairs = [(wd_cc[wb][:, j * 128:(j + 1) * 128],
                                  hidden[:, j * TB3:(j + 1) * TB3], ps[bk][:, :])
                                 for j in range(NIC)]
                        pe_d = mm_group(bk, pairs,
                                        wait_list=[l_wd, ("ve", hid_ready)])
                        wd_free[wb] = ("pe", cnt["pe"])
                        wait("vector", "pe", pe_d)
                        wait("vector", "ve", cwbc_ready)
                        fence("vector")
                        emit("vector", lambda E, bk=bk: E.tensor_tensor(
                            tmp5, ps[bk][:, :], cwbc, OP.mult), inc="ve")
                        fence("vector")
                        oa = out_acc[:, hc * LQ + tb * TB3:hc * LQ + tb * TB3 + TB3]
                        emit("vector", lambda E, oa=oa: E.tensor_tensor(
                            oa, oa, tmp5, OP.add), inc="ve")
                        bank_free[bk] = ("ve", cnt["ve"])
                    hid_free = [("pe", cnt["pe"])] * NIC
            barrier()
            A.release()

        if "2" in phases:
            A.mark()
            K_sb = A.alloc(2 * L)
            V_sb = A.alloc(32 * 256)
            q_sb = A.alloc(2 * LQ)
            S_row = A.alloc(4096)
            P_row = A.alloc(4096)
            PT_sb = [A.alloc(512), A.alloc(512)]
            O_sb = A.alloc(256)

            msk_free = [("ve", 0), ("ve", 0)]
            pt_free = [("pe", 0), ("pe", 0)]
            srow_free = ("se", 0)
            prow_free = ("pe", 0)
            osb_free = ("pe", 0)

            for g in range(G):
                wait("sync", "ve", cnt["ve"])
                wait("sync", "pe", cnt["pe"])
                lK = dma_in(K_sb.rearrange("p (c t) -> p c t", c=2),
                            k_s.rearrange("(c p) t -> p c t", p=128)
                            [:, 2 * g:2 * g + 2, :])
                lV = dma_in(V_sb.rearrange("p (c d) -> p c d", c=32),
                            v_s.rearrange("(c p) d -> p c d", p=128)
                            [:, :, g * 256:(g + 1) * 256])
                lq = dma_in(q_sb.rearrange("p (c t) -> p c t", c=2),
                            q_s.rearrange("(c p) t -> p c t", p=128)
                            [:, 2 * g:2 * g + 2, :])
                for qt in range(16):
                    l0 = qt * 128
                    kt0 = l0 // 512
                    kts = list(range(kt0, 8))
                    wait("vector", "ge", cnt["ge"])
                    emit("vector", lambda E, qt=qt, g=g: E.tensor_scalar(
                        S1(THRQ), dxcols[:, qt:qt + 1], WIN[g], None, OP.add), inc="ve")
                    fence("vector")
                    emit("vector", lambda E, qt=qt: E.tensor_tensor(
                        S1(THRQ), S1(THRQ), qcap[:, qt:qt + 1], OP.min), inc="ve")
                    fence("vector")
                    for ki, kt in enumerate(kts):
                        emit("vector", lambda E, ki=ki, off=float(l0 - kt * 512): E.tensor_scalar(
                            S1(THRK + 1 + ki), S1(THRQ), off, None, OP.add),
                            inc="ve" if ki == len(kts) - 1 else None)
                    fence("vector")
                    thrk_ve = cnt["ve"]
                    for ki, kt in enumerate(kts):
                        m0 = kt * 512
                        var = 1 + (l0 - m0) // 128 if kt == kt0 else 0
                        mb = ki % 2
                        ss, vv = msk_free[mb]
                        wait("gpsimd", ss, vv)
                        if ki == 0:
                            wait("gpsimd", "ve", thrk_ve)
                        emit("gpsimd", lambda E, var=var, mb=mb, ki=ki: E.tensor_scalar(
                            msku[mb][:, :], dmats[:, var * 512:(var + 1) * 512],
                            S1(THRK + 1 + ki), None, OP.is_le), inc="ge")
                        msk_ge = cnt["ge"]
                        bk = ki % 2
                        pairs = [(q_sb[:, d * LQ + l0:d * LQ + l0 + 128],
                                  K_sb[:, d * L + m0:d * L + m0 + 512], ps[bk][:, :])
                                 for d in range(2)]
                        pe_s = mm_group(bk, pairs,
                                        wait_list=[lq, lK] if ki == 0 and qt == 0 else ())
                        wait("vector", "pe", pe_s)
                        wait("vector", "ge", msk_ge)
                        if ki == 0:
                            ss, vv = srow_free
                            wait("vector", ss, vv)
                        emit("vector", lambda E, kt=kt, mb=mb, bk=bk: E.select(
                            S_row[:, kt * 512:(kt + 1) * 512], msku[mb][:, :],
                            ps[bk][:, :], negbig[:, :]), inc="ve")
                        msk_free[mb] = ("ve", cnt["ve"])
                        bank_free[bk] = ("ve", cnt["ve"])
                    sel_done = cnt["ve"]
                    if debug and g == 0 and qt == 0:
                        dma_out(dbg2, S_row, waits=[("ve", sel_done)])
                    used0 = kt0 * 512
                    ss, vv = srow_free
                    wait("vector", ss, vv)
                    fence("vector")
                    emit("vector", lambda E, used0=used0:
                         E.reduce_max(S1(MAXV), S_row[:, used0:4096], axis=AX), inc="ve")
                    fence("vector")
                    emit("vector", lambda E: E.tensor_scalar(
                        S1(EBIAS), S1(MAXV), -ESC, None, OP.mult), inc="ve")
                    ve_bias = cnt["ve"]
                    ss, vv = prow_free
                    wait("scalar", ss, vv)
                    wait("scalar", "ve", ve_bias)
                    emit("scalar", lambda E, used0=used0: E.activation(
                        P_row[:, used0:4096], S_row[:, used0:4096], AF.Exp,
                        bias=S1(EBIAS), scale=ESC, accum_out=S1(SUMV)), inc="se")
                    se_exp = cnt["se"]
                    srow_free = ("se", se_exp)
                    if debug and g == 0 and qt == 0:
                        dma_out(dbg3, P_row, waits=[("se", se_exp)])
                        dma_out(dbg[:, 16:48], sm[:, 0:32], waits=[("se", se_exp)])
                    wait("vector", "se", se_exp)
                    fence("vector")
                    emit("vector", lambda E: E.reciprocal(S1(RECIP), S1(SUMV)), inc="ve")
                    ve_recip = cnt["ve"]
                    ss, vv = osb_free
                    first_pv = True
                    for ki, kt in enumerate(kts):
                        tb_ = 2 + ki % 2
                        for c4 in range(4):
                            kc = kt * 4 + c4
                            transp(tb_, P_row[:, kc * 128:(kc + 1) * 128],
                                   ps[tb_][:, c4 * 128:(c4 + 1) * 128],
                                   wait_list=[("se", se_exp)] if c4 == 0 else (),
                                   bump=(c4 == 3), bank_wait=(c4 == 0))
                        pe_t = cnt["pe"]
                        sbw = ki % 2
                        sss, vvv = pt_free[sbw]
                        wait("scalar", "pe", max(pe_t, vvv if sss == "pe" else 0))
                        if sss != "pe":
                            wait("scalar", sss, vvv)
                        emit("scalar", lambda E, tb_=tb_, sbw=sbw:
                             E.copy(PT_sb[sbw], ps[tb_][:, :]), inc="se")
                        se_pt = cnt["se"]
                        bank_free[tb_] = ("se", se_pt)
                        ob = 4 + qt % 2
                        if first_pv:
                            s0, v0 = bank_free[ob]
                            wait("tensor", s0, v0)
                            if qt == 0:
                                wait("tensor", *lV)
                        wait("tensor", "se", se_pt)
                        for c4 in range(4):
                            kc = kt * 4 + c4
                            st_ = first_pv and c4 == 0
                            sp_ = (ki == len(kts) - 1) and c4 == 3
                            emit("tensor", lambda E, c4=c4, kc=kc, ob=ob, st_=st_, sp_=sp_, sbw=sbw:
                                 E.matmul(ps[ob][:, 0:256], PT_sb[sbw][:, c4 * 128:(c4 + 1) * 128],
                                          V_sb[:, kc * 256:(kc + 1) * 256],
                                          start=st_, stop=sp_, skip_group_check=True),
                                 inc="pe" if sp_ else None)
                        first_pv = False
                        pt_free[sbw] = ("pe", cnt["pe"])
                    prow_free = ("pe", cnt["pe"])
                    pe_pv = cnt["pe"]
                    ob = 4 + qt % 2
                    ss, vv = osb_free
                    wait("vector", ss, vv)
                    wait("vector", "pe", pe_pv)
                    wait("vector", "ve", ve_recip)
                    emit("vector", lambda E, ob=ob: E.tensor_scalar(
                        O_sb, ps[ob][:, 0:256], S1(RECIP), None, OP.mult), inc="ve")
                    ve_o = cnt["ve"]
                    bank_free[ob] = ("ve", ve_o)
                    bk = 6
                    for c2 in range(2):
                        transp(bk, O_sb[:, c2 * 128:(c2 + 1) * 128],
                               ps[bk][:, c2 * 128:(c2 + 1) * 128],
                               wait_list=[("ve", ve_o)] if c2 == 0 else (),
                               bump=(c2 == 1), bank_wait=(c2 == 0))
                    osb_free = ("pe", cnt["pe"])
                    wait("vector", "pe", cnt["pe"])
                    for c2 in range(2):
                        fc = 2 * g + c2
                        oa = out_acc[:, fc * LQ + l0:fc * LQ + l0 + 128]
                        emit("vector", lambda E, oa=oa, c2=c2, bk=bk: E.tensor_tensor(
                            oa, oa, ps[bk][:, c2 * 128:(c2 + 1) * 128], OP.add),
                            inc="ve" if c2 == 1 else None)
                    bank_free[bk] = ("ve", cnt["ve"])
            barrier()
            A.release()

        A.mark()
        stage = [A.alloc(DM), A.alloc(DM)]
        stage_free = [("st0", 0), ("st0", 0)]
        for tl in range(16):
            sb = tl % 2
            for grp in range(2):
                bk = grp
                for c4 in range(4):
                    c = grp * 4 + c4
                    transp(bk, out_acc[:, c * LQ + tl * 128:c * LQ + tl * 128 + 128],
                           ps[bk][:, c4 * 128:(c4 + 1) * 128],
                           bump=(c4 == 3), bank_wait=(c4 == 0))
                pe_t = cnt["pe"]
                wait("vector", *stage_free[sb])
                wait("vector", "pe", pe_t)
                for c4 in range(4):
                    c = grp * 4 + c4
                    emit("vector", lambda E, c=c, c4=c4, bk=bk, sb=sb:
                         E.tensor_copy(stage[sb][:, c * 128:(c + 1) * 128],
                                       ps[bk][:, c4 * 128:(c4 + 1) * 128]),
                         inc="ve" if c4 == 3 else None)
                bank_free[bk] = ("ve", cnt["ve"])
            stage_free[sb] = dma_out(out_d[tl * 128:(tl + 1) * 128, :], stage[sb],
                                     waits=[("ve", cnt["ve"])])
        if debug:
            dma_out(dbg[:, 0:16], dxcols)
        A.release()
        A.release()

        for n in sem_names:
            if n.startswith(("st", "ld")):
                wait("gpsimd", n, cnt[n])

        with nc.Block() as block:
            @block.sync
            def _(E):
                for f in ops["sync"]:
                    f(E)

            @block.tensor
            def _(E):
                for f in ops["tensor"]:
                    f(E)

            @block.vector
            def _(E):
                for f in ops["vector"]:
                    f(E)

            @block.scalar
            def _(E):
                for f in ops["scalar"]:
                    f(E)

            @block.gpsimd
            def _(E):
                for f in ops["gpsimd"]:
                    f(E)

    return nc


def make_inputs(x_b, h, Wq, Wqkv, bqkv, Wop, W2, b2, Wr, Wg, Wu, Wd):
    xc = np.ascontiguousarray(np.roll(x_b, -2048 * h, axis=0))
    J0 = 4095 if h == 0 else 2047
    p = np.arange(128)[:, None]
    i = np.arange(16)[None, :]
    qcap = (J0 - (i * 128 + p)).astype(np.float32)
    dmats = np.empty((128, 5 * 512), np.float32)
    j = np.arange(512)[None, :]
    base = (j - p).astype(np.float32)
    dmats[:, 0:512] = base
    for v in range(4):
        r = v * 128
        m = np.where((j - p) < r, np.float32(1e9), base)
        dmats[:, (v + 1) * 512:(v + 2) * 512] = m
    return {
        "x": xc,
        "qcap": qcap,
        "wqt": np.ascontiguousarray(Wq.T),
        "wqkvt": np.ascontiguousarray(Wqkv.T),
        "bqkv_kc": np.ascontiguousarray(bqkv[1024:2048].reshape(8, 128).T),
        "bqkv_qc": np.ascontiguousarray(bqkv[0:1024].reshape(8, 128).T),
        "bqkv_vb": np.ascontiguousarray(np.broadcast_to(bqkv[2048:3072][None, :], (128, 1024))),
        "wopt": np.ascontiguousarray(Wop.T),
        "w2t": np.ascontiguousarray(W2.reshape(2, 128).T),
        "b2c": np.ascontiguousarray(np.broadcast_to(b2.reshape(1, 1), (128, 1))),
        "ident": np.eye(128, dtype=np.float32),
        "dmats": dmats,
        "wr": Wr,
        "wg": Wg,
        "wu": Wu,
        "wd": Wd,
    }


def kernel(L0_attn_x, L0_attn_Wq, L0_attn_Wk, L0_attn_Wv, L0_attn_Wqkv, L0_attn_bqkv,
           L0_attn_Wop, L0_attn_W2, L0_attn_b2, L0_ffn_Wr, L0_ffn_Wg, L0_ffn_Wu,
           L0_ffn_Wd):
    x = np.ascontiguousarray(np.asarray(L0_attn_x, dtype=np.float32))
    args = [np.ascontiguousarray(np.asarray(a, dtype=np.float32)) for a in
            (L0_attn_Wq, L0_attn_Wqkv, L0_attn_bqkv, L0_attn_Wop, L0_attn_W2,
             L0_attn_b2, L0_ffn_Wr, L0_ffn_Wg, L0_ffn_Wu, L0_ffn_Wd)]
    nc = build()
    in_maps = [make_inputs(x[c // 2], c % 2, *args) for c in range(8)]
    res = run_bass_kernel_spmd(nc, in_maps, core_ids=list(range(8)))
    out = np.empty((4, 4096, 1024), np.float32)
    for c in range(8):
        b, h = c // 2, c % 2
        out[b, 2048 * h:2048 * (h + 1), :] = np.asarray(res.results[c]["out"])
    return out


# version 03
# speedup vs baseline: 2.0170x; 1.9874x over previous
import numpy as np
import concourse.bass as bass
import concourse.mybir as mybir
from concourse.bass_utils import run_bass_kernel_spmd
from concourse.alu_op_type import AluOpType as OP
from concourse.bass import IndirectOffsetOnAxis

F32 = mybir.dt.float32
U8 = mybir.dt.uint8
AF = mybir.ActivationFunctionType
AX = mybir.AxisListType.X

L, DM, LQ = 4096, 1024, 2048
G, DH = 4, 256
WIN = (32.0, 64.0, 128.0, 256.0)
ESC = 2.0 / 64.0
NE, DI = 9, 2816
NIC = DI // 128
NEG = -1.0e18
EPS = 1e-5
CPAD = 640
ENGS = ("sync", "tensor", "vector", "scalar", "gpsimd")
ARENA_W = 52000


def build(phases=("1a", "1b", "3", "2"), debug=False):
    nc = bass.Bass("TRN2", target_bir_lowering=False)

    x_in = nc.dram_tensor("x", [L, DM], F32, kind="ExternalInput").ap()
    qcap_in = nc.dram_tensor("qcap", [128, 16], F32, kind="ExternalInput").ap()
    wqt_in = nc.dram_tensor("wqt", [DM, DM], F32, kind="ExternalInput").ap()
    wqkvt_in = nc.dram_tensor("wqkvt", [DM, 3 * DM], F32, kind="ExternalInput").ap()
    bqkv_kc_in = nc.dram_tensor("bqkv_kc", [128, 8], F32, kind="ExternalInput").ap()
    bqkv_qc_in = nc.dram_tensor("bqkv_qc", [128, 8], F32, kind="ExternalInput").ap()
    bqkv_vb_in = nc.dram_tensor("bqkv_vb", [128, DM], F32, kind="ExternalInput").ap()
    wopt_in = nc.dram_tensor("wopt", [DM, 256], F32, kind="ExternalInput").ap()
    w2t_in = nc.dram_tensor("w2t", [128, 2], F32, kind="ExternalInput").ap()
    b2c_in = nc.dram_tensor("b2c", [128, 1], F32, kind="ExternalInput").ap()
    ident_in = nc.dram_tensor("ident", [128, 128], F32, kind="ExternalInput").ap()
    dmats_in = nc.dram_tensor("dmats", [128, 5 * 512], F32, kind="ExternalInput").ap()
    wr_in = nc.dram_tensor("wr", [DM, NE], F32, kind="ExternalInput").ap()
    wg_in = nc.dram_tensor("wg", [NE, DM, DI], F32, kind="ExternalInput").ap()
    wu_in = nc.dram_tensor("wu", [NE, DM, DI], F32, kind="ExternalInput").ap()
    wd_in = nc.dram_tensor("wd", [NE, DI, DM], F32, kind="ExternalInput").ap()
    tokid_in = nc.dram_tensor("tokid", [128, 16], F32, kind="ExternalInput").ap()
    out_d = nc.dram_tensor("out", [LQ, DM], F32, kind="ExternalOutput").ap()

    skind = dict(kind="ExternalOutput") if debug else {}
    k_s = nc.dram_tensor("k_s", [DM, L], F32, **skind).ap()
    v_s = nc.dram_tensor("v_s", [L, DM], F32, **skind).ap()
    q_s = nc.dram_tensor("q_s", [DM, LQ], F32, **skind).ap()
    qin_s = nc.dram_tensor("qin_s", [DM, LQ], F32, **skind).ap()
    xn_s = nc.dram_tensor("xn_s", [DM, LQ], F32, **skind).ap()
    xn_tm_s = nc.dram_tensor("xn_tm_s", [LQ + 128, DM], F32, **skind).ap()
    acc_s = nc.dram_tensor("acc_s", [LQ + 128, DM], F32, **skind).ap()
    cw_s = nc.dram_tensor("cw_s", [LQ, NE], F32).ap()
    dbg = nc.dram_tensor("dbg", [128, 512], F32, kind="ExternalOutput").ap() if debug else None
    dbgi = nc.dram_tensor("dbgi", [128, 48], mybir.dt.uint32, kind="ExternalOutput").ap() if debug else None
    dbge = nc.dram_tensor("dbge", [9, 2048], F32, kind="ExternalOutput").ap() if debug else None
    dbg2 = nc.dram_tensor("dbg2", [128, 4096], F32, kind="ExternalOutput").ap() if debug else None
    dbg3 = nc.dram_tensor("dbg3", [128, 4096], F32, kind="ExternalOutput").ap() if debug else None

    import contextlib
    es = contextlib.ExitStack()
    with es:
        arena_t = es.enter_context(nc.sbuf_tensor("arena", [128, ARENA_W], F32))
        msku = [es.enter_context(nc.sbuf_tensor(f"msku{i}", [128, 512], U8)) for i in range(2)]
        msk9 = es.enter_context(nc.sbuf_tensor("msk9", [128, 16], U8))
        msk_tu8 = es.enter_context(nc.sbuf_tensor("msk_tu8", [128, 256], U8))
        mrow = es.enter_context(nc.sbuf_tensor("mrow", [128, 640], U8))
        idxu_all = es.enter_context(nc.sbuf_tensor("idxu_all", [128, 48], mybir.dt.uint32))
        ps = [es.enter_context(nc.psum_tensor(f"ps{i}", [128, 512], F32)) for i in range(8)]
        NLD, NST = 16, 8
        sem_names = ([f"ld{i}" for i in range(NLD)] + [f"st{i}" for i in range(NST)]
                     + ["pe", "ve", "se", "ge"])
        sem = {n: es.enter_context(nc.semaphore(f"s_{n}")) for n in sem_names}

        ops = {e: [] for e in ENGS}
        cnt = {n: 0 for n in sem}

        def on(e, f):
            ops[e].append(f)

        def wait(e, s, v):
            if v > 0:
                on(e, lambda E, s=s, v=v: E.wait_ge(sem[s], v))

        def emit(e, builder, inc=None, incn=1):
            if inc is None:
                on(e, builder)
                return None
            cnt[inc] += incn
            on(e, lambda E, b=builder, s=inc, n=incn: b(E).then_inc(sem[s], n))
            return cnt[inc]

        OWN = {"vector": "ve", "scalar": "se", "gpsimd": "ge", "tensor": "pe"}

        def fence(e):
            wait(e, OWN[e], cnt[OWN[e]])

        dma_idx = [0, 0]

        def dma_in(dst, src):
            name = f"ld{dma_idx[0] % NLD}"
            dma_idx[0] += 1
            emit("sync", lambda E, d=dst, s=src: E.dma_start(out=d, in_=s),
                 inc=name, incn=16)
            return (name, cnt[name])

        def dma_out(dst, src, waits=()):
            for s, v in waits:
                wait("gpsimd", s, v)
            name = f"st{dma_idx[1] % NST}"
            dma_idx[1] += 1
            emit("gpsimd", lambda E, d=dst, s=src: E.dma_start(out=d, in_=s),
                 inc=name, incn=16)
            return (name, cnt[name])

        def barrier():
            snap = dict(cnt)
            own = {"tensor": ("pe",), "vector": ("ve",), "scalar": ("se",),
                   "gpsimd": ("ge",), "sync": ()}
            for e in ENGS:
                for s, v in snap.items():
                    if s not in own[e]:
                        wait(e, s, v)

        class Arena:
            def __init__(self):
                self.off = 0
                self.stack = []

            def alloc(self, w):
                w16 = (w + 15) // 16 * 16
                assert self.off + w16 <= ARENA_W, f"arena overflow: {self.off + w16}"
                ap = arena_t[:, self.off:self.off + w]
                self.off += w16
                return ap

            def mark(self):
                self.stack.append(self.off)

            def release(self):
                self.off = self.stack.pop()

        A = Arena()

        ident = A.alloc(128)
        dmats = A.alloc(5 * 512)
        negbig = A.alloc(512)
        qcap = A.alloc(16)
        b2c = A.alloc(16)[:, 0:1]
        w2t = A.alloc(16)
        ones = A.alloc(128)
        tokid_sb = A.alloc(16)
        dxcols = A.alloc(16)
        sm = A.alloc(64)

        def S1(i):
            return sm[:, i:i + 1]

        XSUM, SUMSQ, MEAN, VAR, SD, RSIG, NB = 0, 1, 2, 3, 4, 5, 6
        MAXV, EBIAS, SUMV, RECIP, THRQ, THRK = 7, 8, 9, 10, 11, 12
        M1, M2, D12, W1, W2, SGT = 13, 14, 15, 16, 17, 18

        consts_toks = [dma_in(ident, ident_in), dma_in(dmats, dmats_in),
                       dma_in(qcap, qcap_in), dma_in(b2c, b2c_in),
                       dma_in(w2t[:, 0:2], w2t_in[:, 0:2]),
                       dma_in(tokid_sb, tokid_in)]
        emit("gpsimd", lambda E: E.memset(negbig[:, :], NEG), inc="ge")
        emit("gpsimd", lambda E: E.memset(ones[0:1, :], 1.0), inc="ge")

        bank_free = [("ve", 0)] * 8

        def mm_group(bank, pairs, wait_list=(), bump=True):
            s, v = bank_free[bank]
            wait("tensor", s, v)
            for ss, vv in wait_list:
                wait("tensor", ss, vv)
            n = len(pairs)
            for i, (lh, rh, outap) in enumerate(pairs):
                inc = "pe" if (bump and i == n - 1) else None
                emit("tensor",
                     lambda E, o=outap, l=lh, r=rh, st=(i == 0), sp=(i == n - 1):
                     E.matmul(o, l, r, start=st, stop=sp, skip_group_check=True),
                     inc=inc)
            return cnt["pe"]

        def transp(bank, in_ap, out_ap, wait_list=(), bump=True, bank_wait=True,
                   id9=False, id1=False):
            if bank_wait:
                s, v = bank_free[bank]
                wait("tensor", s, v)
            for ss, vv in wait_list:
                wait("tensor", ss, vv)
            idap = ident[0:NE, 0:NE] if id9 else (ident[0:1, 0:1] if id1 else ident)
            return emit("tensor",
                        lambda E, o=out_ap, i=in_ap, idap=idap: E.transpose(o, i, idap),
                        inc="pe" if bump else None)

        if "1a" in phases:
            A.mark()
            wqt = A.alloc(8 * DM)
            wqkvt = A.alloc(8 * 3 * DM)
            bqkv_kc = A.alloc(16)
            bqkv_qc = A.alloc(16)
            bqkv_vb = A.alloc(DM)
            x_t = [A.alloc(DM), A.alloc(DM)]
            xn_t = [A.alloc(DM), A.alloc(DM)]
            xn_fm = A.alloc(8 * 512)
            qin_fm = A.alloc(8 * 512)
            stg = [A.alloc(512), A.alloc(512)]

            w_toks = [dma_in(wqt.rearrange("p (c o) -> p c o", c=8),
                             wqt_in.rearrange("(c p) o -> p c o", p=128)),
                      dma_in(wqkvt.rearrange("p (c o) -> p c o", c=8),
                             wqkvt_in.rearrange("(c p) o -> p c o", p=128)),
                      dma_in(bqkv_kc[:, 0:8], bqkv_kc_in),
                      dma_in(bqkv_qc[:, 0:8], bqkv_qc_in),
                      dma_in(bqkv_vb, bqkv_vb_in)]
            for t_ in consts_toks:
                wait("tensor", *t_)

            xbuf_free = [("se", 0), ("se", 0)]
            xnt_dma = [("st0", 0), ("st0", 0)]
            xnfm_dma_st = ("st0", 0)
            qinfm_dma_st = ("st0", 0)
            xnbuf_free = [("pe", 0), ("pe", 0)]
            stg_free = [("st0", 0), ("st0", 0)]
            stats_read = 0

            for tb in range(8):
                own = tb < 4
                for s4 in range(4):
                    t0 = tb * 512 + s4 * 128
                    bi = s4 % 2
                    xe, xne = x_t[bi], xn_t[bi]
                    ss, vv = xbuf_free[bi]
                    wait("sync", ss, vv)
                    l = dma_in(xe, x_in[t0:t0 + 128, :])
                    ss, vv = xnbuf_free[bi]
                    wait("scalar", ss, vv)
                    wait("scalar", *xnt_dma[bi])
                    wait("scalar", *l)
                    wait("scalar", "ve", stats_read)
                    emit("scalar", lambda E, xn=xne, x=xe:
                         E.activation(xn, x, AF.Square, accum_out=S1(SUMSQ)), inc="se")
                    se_sq = cnt["se"]
                    wait("vector", *l)
                    emit("vector", lambda E, x=xe: E.reduce_sum(S1(XSUM), x, axis=AX), inc="ve")
                    fence("vector")
                    wait("vector", "se", se_sq)
                    emit("vector", lambda E: E.tensor_scalar(S1(MEAN), S1(XSUM), 1.0 / DM, None, OP.mult), inc="ve")
                    emit("vector", lambda E: E.tensor_scalar(S1(VAR), S1(SUMSQ), 1.0 / DM, None, OP.mult), inc="ve")
                    fence("vector")
                    emit("vector", lambda E: E.tensor_tensor(S1(SGT), S1(MEAN), S1(MEAN), OP.mult), inc="ve")
                    fence("vector")
                    emit("vector", lambda E: E.scalar_tensor_tensor(
                        S1(SGT), S1(SGT), -1.0, S1(VAR), OP.mult, OP.add), inc="ve")
                    fence("vector")
                    emit("vector", lambda E: E.tensor_scalar(S1(VAR), S1(SGT), EPS, None, OP.add), inc="ve")
                    ve_var = cnt["ve"]
                    wait("scalar", "ve", ve_var)
                    emit("scalar", lambda E: E.activation(S1(SD), S1(VAR), AF.Sqrt), inc="se")
                    se_sd = cnt["se"]
                    wait("vector", "se", se_sd)
                    emit("vector", lambda E: E.reciprocal(S1(RSIG), S1(SD)), inc="ve")
                    fence("vector")
                    emit("vector", lambda E: E.scalar_tensor_tensor(
                        S1(NB), S1(MEAN), -1.0, S1(RSIG), OP.mult, OP.mult), inc="ve")
                    fence("vector")
                    stats_read = cnt["ve"]
                    ss, vv = xnbuf_free[bi]
                    wait("vector", ss, vv)
                    emit("vector", lambda E, xn=xne, x=xe: E.tensor_scalar(
                        xn, x, S1(RSIG), S1(NB), OP.mult, OP.add), inc="ve")
                    ve_xn = cnt["ve"]
                    xbuf_free[bi] = ("ve", ve_xn)
                    if own:
                        xnt_dma[bi] = dma_out(xn_tm_s[t0:t0 + 128, :], xne,
                                              waits=[("ve", ve_xn)])
                    if s4 == 0:
                        wait("vector", *xnfm_dma_st)
                    for grp in range(2):
                        bk = grp
                        wl = [("ve", ve_xn)]
                        for c4 in range(4):
                            c = grp * 4 + c4
                            transp(bk, xne[:, c * 128:(c + 1) * 128],
                                   ps[bk][:, c4 * 128:(c4 + 1) * 128],
                                   wait_list=wl if c4 == 0 else (),
                                   bump=(c4 == 3), bank_wait=(c4 == 0))
                        pe_t = cnt["pe"]
                        wait("vector", "pe", pe_t)
                        for c4 in range(4):
                            c = grp * 4 + c4
                            emit("vector", lambda E, c=c, c4=c4, bk=bk, s4=s4:
                                 E.tensor_copy(xn_fm[:, c * 512 + s4 * 128:c * 512 + s4 * 128 + 128],
                                               ps[bk][:, c4 * 128:(c4 + 1) * 128]),
                                 inc="ve" if c4 == 3 else None)
                        bank_free[bk] = ("ve", cnt["ve"])
                    xnbuf_free[bi] = ("pe", cnt["pe"])
                xnfm_ready = cnt["ve"]

                if own:
                    xnfm_dma_st = dma_out(
                        xn_s.rearrange("(c p) t -> p c t", p=128)[:, :, tb * 512:(tb + 1) * 512],
                        xn_fm.rearrange("p (c t) -> p c t", c=8),
                        waits=[("ve", xnfm_ready)])

                wait("vector", *qinfm_dma_st)
                for oc in range(8):
                    bk = 2 + oc % 2
                    pairs = [(wqt[:, c * DM + oc * 128:c * DM + oc * 128 + 128],
                              xn_fm[:, c * 512:(c + 1) * 512], ps[bk][:, :]) for c in range(8)]
                    pe_g = mm_group(bk, pairs,
                                    wait_list=[("ve", xnfm_ready)] + w_toks)
                    wait("vector", "pe", pe_g)
                    emit("vector", lambda E, oc=oc, bk=bk:
                         E.tensor_copy(qin_fm[:, oc * 512:(oc + 1) * 512], ps[bk][:, :]), inc="ve")
                    bank_free[bk] = ("ve", cnt["ve"])
                qinfm_ready = cnt["ve"]
                if own:
                    qinfm_dma_st = dma_out(
                        qin_s.rearrange("(c p) t -> p c t", p=128)[:, :, tb * 512:(tb + 1) * 512],
                        qin_fm.rearrange("p (c t) -> p c t", c=8),
                        waits=[("ve", qinfm_ready)])

                dsts = [("k", 1024, bqkv_kc, k_s, L, tb * 512)]
                if own:
                    dsts.append(("q", 0, bqkv_qc, q_s, LQ, tb * 512))
                for name, obase, bias_c, dstt, dw, tcol in dsts:
                    for oc in range(8):
                        bk = 2 + oc % 2
                        pairs = [(wqkvt[:, c * 3 * DM + obase + oc * 128:
                                        c * 3 * DM + obase + oc * 128 + 128],
                                  qin_fm[:, c * 512:(c + 1) * 512], ps[bk][:, :])
                                 for c in range(8)]
                        pe_g = mm_group(bk, pairs,
                                        wait_list=[("ve", qinfm_ready)] + w_toks)
                        sb = oc % 2
                        wait("vector", *stg_free[sb])
                        for t_ in w_toks:
                            wait("vector", *t_)
                        wait("vector", "pe", pe_g)
                        emit("vector", lambda E, bk=bk, sb=sb, bias_c=bias_c, oc=oc:
                             E.tensor_scalar(stg[sb], ps[bk][:, :],
                                             bias_c[:, oc:oc + 1], None, OP.add), inc="ve")
                        bank_free[bk] = ("ve", cnt["ve"])
                        stg_free[sb] = dma_out(dstt[oc * 128:(oc + 1) * 128, tcol:tcol + 512],
                                               stg[sb], waits=[("ve", cnt["ve"])])

                for s4 in range(4):
                    for half in range(2):
                        bk = 4 + half
                        pairs = [(qin_fm[:, c * 512 + s4 * 128:c * 512 + s4 * 128 + 128],
                                  wqkvt[:, c * 3 * DM + 2048 + half * 512:
                                        c * 3 * DM + 2048 + half * 512 + 512], ps[bk][:, :])
                                 for c in range(8)]
                        pe_g = mm_group(bk, pairs,
                                        wait_list=[("ve", qinfm_ready)] + w_toks)
                        sb = half
                        wait("vector", *stg_free[sb])
                        for t_ in w_toks:
                            wait("vector", *t_)
                        wait("vector", "pe", pe_g)
                        emit("vector", lambda E, bk=bk, sb=sb, half=half:
                             E.scalar_tensor_tensor(stg[sb], ps[bk][:, :], 1.0,
                                                    bqkv_vb[:, half * 512:half * 512 + 512],
                                                    OP.mult, OP.add), inc="ve")
                        bank_free[bk] = ("ve", cnt["ve"])
                        stg_free[sb] = dma_out(v_s[tb * 512 + s4 * 128:tb * 512 + s4 * 128 + 128,
                                                   half * 512:half * 512 + 512], stg[sb],
                                               waits=[("ve", cnt["ve"])])
            barrier()
            A.release()

        if "1b" in phases:
            A.mark()
            wopt = A.alloc(8 * 256)
            qin_b = A.alloc(8 * 512)
            relu_b = A.alloc(8 * 512)
            h_fm = A.alloc(2 * LQ)
            wop_tok = dma_in(wopt.rearrange("p (c o) -> p c o", c=8),
                             wopt_in.rearrange("(c p) o -> p c o", p=128))
            relu_read = 0
            for blk in range(4):
                wait("sync", "se", relu_read)
                l = dma_in(qin_b.rearrange("p (c t) -> p c t", c=8),
                           qin_s.rearrange("(c p) t -> p c t", p=128)[:, :, blk * 512:(blk + 1) * 512])
                wait("scalar", *l)
                wait("scalar", "pe", cnt["pe"])
                emit("scalar", lambda E: E.activation(relu_b, qin_b, AF.Relu), inc="se")
                relu_read = cnt["se"]
                for oc in range(2):
                    bk = oc
                    pairs = [(wopt[:, c * 256 + oc * 128:c * 256 + oc * 128 + 128],
                              relu_b[:, c * 512:(c + 1) * 512], ps[bk][:, :]) for c in range(8)]
                    pe_g = mm_group(bk, pairs, wait_list=[("se", relu_read), wop_tok])
                    wait("vector", "pe", pe_g)
                    emit("vector", lambda E, oc=oc, blk=blk, bk=bk:
                         E.tensor_copy(h_fm[:, oc * LQ + blk * 512:oc * LQ + blk * 512 + 512],
                                       ps[bk][:, :]), inc="ve")
                    bank_free[bk] = ("ve", cnt["ve"])
            h_ready = cnt["ve"]
            for tc in range(16):
                bk = 2 + tc % 2
                pairs = [(h_fm[:, oc * LQ + tc * 128:oc * LQ + tc * 128 + 128],
                          w2t[:, oc:oc + 1], ps[bk][:, 0:1]) for oc in range(2)]
                pe_g = mm_group(bk, pairs, wait_list=[("ve", h_ready)])
                wait("scalar", "pe", pe_g)
                wait("scalar", "ve", cnt["ve"])
                emit("scalar", lambda E, bk=bk: E.activation(
                    S1(SGT), ps[bk][:, 0:1], AF.Sigmoid, bias=b2c, scale=1.0), inc="se")
                wait("vector", "se", cnt["se"])
                emit("vector", lambda E, tc=tc: E.tensor_scalar(
                    dxcols[:, tc:tc + 1], S1(SGT), float(L), None, OP.mult), inc="ve")
                bank_free[bk] = ("ve", cnt["ve"])
            barrier()
            A.release()

        if "3" in phases:
            A.mark()
            xn_b = A.alloc(8 * 512)
            wr_sb = A.alloc(8 * 16)
            logits = A.alloc(16 * 16)
            cw_cols = A.alloc(16 * 16)
            tmp5 = A.alloc(512)
            mskf = A.alloc(16 * 16)
            msk_em = A.alloc(2048)
            pos_em = A.alloc(2048)
            pos_tm = A.alloc(16 * 16)
            target = A.alloc(16 * 16)
            negones = A.alloc(16 * 16)
            iotaC = A.alloc(CPAD)
            trashr = A.alloc(CPAD)
            cmpt = [A.alloc(CPAD), A.alloc(CPAD)]
            idxrow = A.alloc(CPAD)
            idxrow2 = A.alloc(CPAD)
            target2 = A.alloc(16 * 16)
            idxcolf = A.alloc(16)
            crow = A.alloc(16)

            wr_tok = dma_in(wr_sb.rearrange("p (c e) -> p c e", c=8, e=16)[:, :, 0:NE],
                            wr_in.rearrange("(c p) e -> p c e", p=128))
            for tb in range(4):
                wait("sync", "pe", cnt["pe"])
                l = dma_in(xn_b.rearrange("p (c t) -> p c t", c=8),
                           xn_s.rearrange("(c p) t -> p c t", p=128)[:, :, tb * 512:(tb + 1) * 512])
                for tc4 in range(4):
                    tc = tb * 4 + tc4
                    bk = 2 + tc4 % 2
                    pairs = [(xn_b[:, c * 512 + tc4 * 128:c * 512 + tc4 * 128 + 128],
                              wr_sb[:, c * 16:c * 16 + NE], ps[bk][:, 0:NE]) for c in range(8)]
                    pe_g = mm_group(bk, pairs, wait_list=[l, wr_tok])
                    wait("vector", "pe", pe_g)
                    emit("vector", lambda E, tc=tc, bk=bk:
                         E.tensor_copy(logits[:, tc * 16:tc * 16 + NE], ps[bk][:, 0:NE]), inc="ve")
                    bank_free[bk] = ("ve", cnt["ve"])
            for tc in range(16):
                lg = logits[:, tc * 16:tc * 16 + NE]
                emit("vector", lambda E, lg=lg: E.reduce_max(S1(M1), lg, axis=AX), inc="ve")
                fence("vector")
                emit("vector", lambda E, lg=lg: E.tensor_scalar(
                    msk9[:, 0:NE], lg, S1(M1), None, OP.is_equal), inc="ve")
                fence("vector")
                emit("vector", lambda E, lg=lg: E.select(
                    tmp5[:, 0:NE], msk9[:, 0:NE], negbig[:, 0:NE], lg), inc="ve")
                fence("vector")
                emit("vector", lambda E: E.reduce_max(S1(M2), tmp5[:, 0:NE], axis=AX), inc="ve")
                fence("vector")
                emit("vector", lambda E: E.tensor_tensor(S1(D12), S1(M1), S1(M2), OP.subtract),
                     inc="ve")
                wait("scalar", "ve", cnt["ve"])
                emit("scalar", lambda E: E.activation(S1(W1), S1(D12), AF.Sigmoid), inc="se")
                wait("vector", "se", cnt["se"])
                emit("vector", lambda E: E.tensor_scalar(S1(W2), S1(W1), -1.0, 1.0, OP.mult, OP.add), inc="ve")
                emit("vector", lambda E, lg=lg: E.tensor_scalar(
                    tmp5[:, 0:NE], lg, S1(M1), S1(W1), OP.is_equal, OP.mult), inc="ve")
                fence("vector")
                emit("vector", lambda E, lg=lg: E.tensor_scalar(
                    tmp5[:, 16:16 + NE], lg, S1(M2), S1(W2), OP.is_equal, OP.mult), inc="ve")
                fence("vector")
                emit("vector", lambda E, tc=tc: E.tensor_tensor(
                    cw_cols[:, tc * 16:tc * 16 + NE], tmp5[:, 0:NE], tmp5[:, 16:16 + NE], OP.add),
                    inc="ve")
            fence("vector")
            cw_ready = cnt["ve"]
            dma_out(cw_s.rearrange("(tc p) e -> p tc e", p=128),
                    cw_cols.rearrange("p (tc e) -> p tc e", tc=16)[:, :, 0:NE],
                    waits=[("ve", cw_ready)])

            emit("vector", lambda E: E.tensor_scalar(mskf, cw_cols, 0.0, None, OP.is_gt), inc="ve")
            emit("vector", lambda E: E.tensor_scalar(msk_tu8[:, :], cw_cols, 0.0, None, OP.is_gt), inc="ve")
            emit("vector", lambda E: E.memset(negones, -1.0), inc="ve")
            fence("vector")
            mskf_ve = cnt["ve"]
            for grp in range(4):
                bk = grp % 2
                for q4 in range(4):
                    tc = grp * 4 + q4
                    transp(bk, mskf[:, tc * 16:tc * 16 + NE],
                           ps[bk][0:NE, q4 * 128:(q4 + 1) * 128],
                           wait_list=[("ve", mskf_ve)] if q4 == 0 else (),
                           bump=(q4 == 3), bank_wait=(q4 == 0))
                wait("vector", "pe", cnt["pe"])
                emit("vector", lambda E, grp=grp, bk=bk: E.tensor_copy(
                    msk_em[0:NE, grp * 512:(grp + 1) * 512], ps[bk][0:NE, 0:512]), inc="ve")
                bank_free[bk] = ("ve", cnt["ve"])
            fence("vector")
            emit("vector", lambda E: E.memset(sm[0:NE, SGT:SGT + 1], 0.0), inc="ve")
            fence("vector")
            emit("vector", lambda E: E.tensor_tensor_scan(
                pos_em[0:NE, :], msk_em[0:NE, :], msk_em[0:NE, :], sm[0:NE, SGT:SGT + 1],
                OP.add, OP.bypass), inc="ve")
            fence("vector")
            pos_ve = cnt["ve"]
            bk = 6
            transp(bk, pos_em[0:NE, 2047:2048], ps[bk][0:1, 0:NE],
                   wait_list=[("ve", pos_ve)], id9=True)
            wait("vector", "pe", cnt["pe"])
            emit("vector", lambda E, bk=bk: E.tensor_copy(crow[0:1, 0:NE], ps[bk][0:1, 0:NE]), inc="ve")
            bank_free[bk] = ("ve", cnt["ve"])
            for tc in range(16):
                bk = tc % 2
                transp(bk, pos_em[0:NE, tc * 128:(tc + 1) * 128], ps[bk][:, 0:NE],
                       wait_list=[("ve", pos_ve)] if tc < 2 else (), id9=True)
                wait("vector", "pe", cnt["pe"])
                emit("vector", lambda E, tc=tc, bk=bk: E.tensor_copy(
                    pos_tm[:, tc * 16:tc * 16 + NE], ps[bk][:, 0:NE]), inc="ve")
                bank_free[bk] = ("ve", cnt["ve"])
            fence("vector")
            emit("vector", lambda E: E.tensor_scalar(target, pos_tm, -1.0, None, OP.add), inc="ve")
            fence("vector")
            emit("vector", lambda E: E.select(target2, msk_tu8[:, :], target, negones), inc="ve")
            fence("vector")
            target_ve = cnt["ve"]
            emit("gpsimd", lambda E: E.iota(iotaC, [[1, CPAD]], channel_multiplier=0,
                                            allow_small_or_imprecise_dtypes=True), inc="ge")
            emit("gpsimd", lambda E: E.iota(trashr.rearrange("p (c x) -> p c x", c=5),
                                            [[0, 5], [1, 128]], base=2048, channel_multiplier=0,
                                            allow_small_or_imprecise_dtypes=True), inc="ge")
            iota_ge = cnt["ge"]
            wait("vector", "ge", iota_ge)
            cmp_war = [("pe", 0), ("pe", 0)]
            for e in range(NE):
                for tc in range(16):
                    cb = cmpt[tc % 2]
                    wait("vector", *cmp_war[tc % 2])
                    if tc == 0:
                        wait("vector", "ve", target_ve)
                    emit("vector", lambda E, cb=cb, tc=tc, e=e: E.tensor_scalar(
                        cb, iotaC, target2[:, tc * 16 + e:tc * 16 + e + 1], None, OP.is_equal),
                        inc="ve")
                    ve_cb = cnt["ve"]
                    if tc == 0:
                        for bk in (6, 7):
                            sfree, vfree = bank_free[bk]
                            wait("tensor", sfree, vfree)
                    wait("tensor", "ve", ve_cb)
                    emit("tensor", lambda E, cb=cb, tc=tc, st_=(tc == 0):
                         E.matmul(ps[6][0:1, 0:512], tokid_sb[:, tc:tc + 1], cb[:, 0:512],
                                  start=st_, stop=(tc == 15), skip_group_check=True),
                         inc=None)
                    emit("tensor", lambda E, cb=cb, tc=tc, st_=(tc == 0):
                         E.matmul(ps[7][0:1, 0:128], tokid_sb[:, tc:tc + 1], cb[:, 512:CPAD],
                                  start=st_, stop=(tc == 15), skip_group_check=True),
                         inc="pe")
                    cmp_war[tc % 2] = ("pe", cnt["pe"])
                pe_idx = cnt["pe"]
                wait("vector", "pe", pe_idx)
                emit("vector", lambda E: E.tensor_copy(idxrow[0:1, 0:512], ps[6][0:1, 0:512]), inc="ve")
                emit("vector", lambda E: E.tensor_copy(idxrow[0:1, 512:CPAD], ps[7][0:1, 0:128]), inc="ve")
                fence("vector")
                if debug and e == 0:
                    dma_out(dbg2[0:1, 0:CPAD], idxrow[0:1, :], waits=[("ve", cnt["ve"])])
                    dma_out(dbg2[:, 2048:2048 + CPAD], cmpt[15 % 2], waits=[("ve", cnt["ve"])])
                    dma_out(dbg2[:, 3072:3088], tokid_sb, waits=[("ve", cnt["ve"])])
                bank_free[6] = ("ve", cnt["ve"])
                bank_free[7] = ("ve", cnt["ve"])
                emit("vector", lambda E, e=e: E.tensor_scalar(
                    mrow[0:1, :], iotaC[0:1, :], crow[0:1, e:e + 1], None, OP.is_lt), inc="ve")
                fence("vector")
                emit("vector", lambda E: E.select(idxrow2[0:1, :], mrow[0:1, :],
                                                  idxrow[0:1, :], trashr[0:1, :]), inc="ve")
                fence("vector")
                idxrow_ve = cnt["ve"]
                bk = 4 + e % 2
                for c in range(5):
                    transp(bk, idxrow2[0:1, c * 128:(c + 1) * 128], ps[bk][:, c:c + 1],
                           wait_list=[("ve", idxrow_ve)] if c == 0 else (),
                           bump=(c == 4), bank_wait=(c == 0), id1=True)
                wait("vector", "pe", cnt["pe"])
                emit("vector", lambda E, bk=bk: E.tensor_copy(idxcolf[:, 0:5], ps[bk][:, 0:5]), inc="ve")
                fence("vector")
                bank_free[bk] = ("ve", cnt["ve"])
                emit("vector", lambda E, e=e: E.tensor_copy(
                    idxu_all[:, e * 5:(e + 1) * 5], idxcolf[:, 0:5]), inc="ve")
            idx_ready = cnt["ve"]
            if debug:
                dma_out(dbgi, idxu_all[:, :], waits=[("ve", idx_ready)])
                dma_out(dbg[:, 64:80], crow[:, 0:16], waits=[("ve", idx_ready)])
                dma_out(dbg[:, 80:224], pos_tm[:, 0:144], waits=[("ve", idx_ready)])
                dma_out(dbg[:, 224:368], target2[:, 0:144], waits=[("ve", idx_ready)])
                dma_out(dbge, pos_em[0:9, :], waits=[("ve", idx_ready)])
            barrier()
            A.release()

            A.mark()
            xg_tm = A.alloc(5 * DM)
            xg_fm = A.alloc(8 * CPAD)
            hidden = A.alloc(NIC * CPAD)
            wg_sl = [A.alloc(8 * 128), A.alloc(8 * 128)]
            wu_sl = [A.alloc(8 * 128), A.alloc(8 * 128)]
            wd_cc = [A.alloc(NIC * 128), A.alloc(NIC * 128)]
            gate_sb = A.alloc(CPAD)
            down_fm = A.alloc(8 * CPAD)
            stage2 = [A.alloc(DM), A.alloc(DM)]
            w_sl = A.alloc(8)
            zz = A.alloc(DM)

            emit("vector", lambda E: E.memset(zz, 0.0), inc="ve")
            zz_ve = cnt["ve"]
            zinit = []
            for i in range(17):
                zinit.append(dma_out(acc_s[i * 128:(i + 1) * 128, :], zz, waits=[("ve", zz_ve)] if i == 0 else ()))
            ztrash = dma_out(xn_tm_s[2048:2176, :], zz)

            wgu_free = [("pe", 0), ("pe", 0)]
            wd_free = [("pe", 0), ("pe", 0)]
            gate_free = ("ve", 0)
            xgtm_free = ("pe", 0)
            wsl_free = ("ve", 0)
            stage_free = [("st0", 0), ("st0", 0)]
            sc_last = ztrash
            hid_free = ("pe", 0)
            dfm_free = ("pe", 0)
            for e in range(NE):
                wait("gpsimd", *xgtm_free)
                if e == 0:
                    wait("gpsimd", "ve", idx_ready)
                    wait("gpsimd", *ztrash)
                g_toks = []
                for c in range(5):
                    name = f"ld{dma_idx[0] % NLD}"
                    dma_idx[0] += 1
                    emit("gpsimd", lambda E, c=c, e=e: E.indirect_dma_start(
                        out=xg_tm[:, c * DM:(c + 1) * DM], out_offset=None,
                        in_=xn_tm_s,
                        in_offset=IndirectOffsetOnAxis(ap=idxu_all[:, e * 5 + c:e * 5 + c + 1], axis=0)),
                        inc=name, incn=16)
                    g_toks.append((name, cnt[name]))
                wait("gpsimd", *wsl_free)
                w_toks2 = []
                for c in range(5):
                    name = f"ld{dma_idx[0] % NLD}"
                    dma_idx[0] += 1
                    emit("gpsimd", lambda E, c=c, e=e: E.indirect_dma_start(
                        out=w_sl[:, c:c + 1], out_offset=None,
                        in_=cw_s, element_offset=e,
                        in_offset=IndirectOffsetOnAxis(ap=idxu_all[:, e * 5 + c:e * 5 + c + 1], axis=0)),
                        inc=name, incn=16)
                    w_toks2.append((name, cnt[name]))
                for c in range(5):
                    for grp in range(2):
                        bk = grp
                        for c4 in range(4):
                            fc = grp * 4 + c4
                            transp(bk, xg_tm[:, c * DM + fc * 128:c * DM + fc * 128 + 128],
                                   ps[bk][:, c4 * 128:(c4 + 1) * 128],
                                   wait_list=[g_toks[c]] if c4 == 0 else (),
                                   bump=(c4 == 3), bank_wait=(c4 == 0))
                        pe_t = cnt["pe"]
                        wait("vector", "pe", pe_t)
                        for c4 in range(4):
                            fc = grp * 4 + c4
                            emit("vector", lambda E, fc=fc, c4=c4, bk=bk, c=c:
                                 E.tensor_copy(xg_fm[:, fc * CPAD + c * 128:fc * CPAD + c * 128 + 128],
                                               ps[bk][:, c4 * 128:(c4 + 1) * 128]),
                                 inc="ve" if c4 == 3 else None)
                        bank_free[bk] = ("ve", cnt["ve"])
                xg_ready = cnt["ve"]
                xgtm_free = ("pe", cnt["pe"])

                wait("tensor", *hid_free)
                for ic in range(NIC):
                    wb = ic % 2
                    wait("sync", *wgu_free[wb])
                    lg_ = dma_in(wg_sl[wb].rearrange("p (c i) -> p c i", c=8),
                                 wg_in[e].rearrange("(c p) i -> p c i", p=128)
                                 [:, :, ic * 128:(ic + 1) * 128])
                    lu_ = dma_in(wu_sl[wb].rearrange("p (c i) -> p c i", c=8),
                                 wu_in[e].rearrange("(c p) i -> p c i", p=128)
                                 [:, :, ic * 128:(ic + 1) * 128])
                    for nb, (n0, nw, bkg, bku) in enumerate([(0, 512, 0, 2), (512, 128, 1, 3)]):
                        pairs = [(wg_sl[wb][:, c * 128:(c + 1) * 128],
                                  xg_fm[:, c * CPAD + n0:c * CPAD + n0 + nw], ps[bkg][:, 0:nw])
                                 for c in range(8)]
                        pe_gate = mm_group(bkg, pairs,
                                           wait_list=[lg_, ("ve", xg_ready)] if nb == 0 else [lg_])
                        ss, vv = gate_free
                        if nb == 0:
                            wait("scalar", ss, vv)
                        wait("scalar", "pe", pe_gate)
                        emit("scalar", lambda E, bkg=bkg, n0=n0, nw=nw:
                             E.activation(gate_sb[:, n0:n0 + nw], ps[bkg][:, 0:nw], AF.Silu), inc="se")
                        bank_free[bkg] = ("se", cnt["se"])
                        pairs = [(wu_sl[wb][:, c * 128:(c + 1) * 128],
                                  xg_fm[:, c * CPAD + n0:c * CPAD + n0 + nw], ps[bku][:, 0:nw])
                                 for c in range(8)]
                        pe_up = mm_group(bku, pairs, wait_list=[lu_])
                        wait("vector", "pe", pe_up)
                        wait("vector", "se", cnt["se"])
                        emit("vector", lambda E, ic=ic, bku=bku, n0=n0, nw=nw: E.tensor_tensor(
                            hidden[:, ic * CPAD + n0:ic * CPAD + n0 + nw],
                            gate_sb[:, n0:n0 + nw], ps[bku][:, 0:nw], OP.mult), inc="ve")
                        bank_free[bku] = ("ve", cnt["ve"])
                    gate_free = ("ve", cnt["ve"])
                    wgu_free[wb] = ("pe", cnt["pe"])
                hid_ready = cnt["ve"]

                wait("tensor", *dfm_free)
                for hc in range(8):
                    wb = hc % 2
                    wait("sync", *wd_free[wb])
                    l_wd = dma_in(wd_cc[wb].rearrange("p (c h) -> p c h", c=NIC),
                                  wd_in[e].rearrange("(c p) h -> p c h", p=128)
                                  [:, :, hc * 128:(hc + 1) * 128])
                    for nb, (n0, nw) in enumerate([(0, 512), (512, 128)]):
                        bk = 4 + (2 * hc + nb) % 2
                        pairs = [(wd_cc[wb][:, j * 128:(j + 1) * 128],
                                  hidden[:, j * CPAD + n0:j * CPAD + n0 + nw], ps[bk][:, 0:nw])
                                 for j in range(NIC)]
                        pe_d = mm_group(bk, pairs, wait_list=[l_wd, ("ve", hid_ready)])
                        wait("vector", "pe", pe_d)
                        emit("vector", lambda E, hc=hc, bk=bk, n0=n0, nw=nw: E.tensor_copy(
                            down_fm[:, hc * CPAD + n0:hc * CPAD + n0 + nw], ps[bk][:, 0:nw]), inc="ve")
                        bank_free[bk] = ("ve", cnt["ve"])
                    wd_free[wb] = ("pe", cnt["pe"])
                dfm_ready = cnt["ve"]

                for sc in range(5):
                    sb2 = sc % 2
                    for grp in range(2):
                        bk = 6 + grp
                        for c4 in range(4):
                            hc = grp * 4 + c4
                            transp(bk, down_fm[:, hc * CPAD + sc * 128:hc * CPAD + sc * 128 + 128],
                                   ps[bk][:, c4 * 128:(c4 + 1) * 128],
                                   wait_list=[("ve", dfm_ready)] if c4 == 0 and grp == 0 else (),
                                   bump=(c4 == 3), bank_wait=(c4 == 0))
                        pe_t = cnt["pe"]
                        if grp == 0:
                            wait("vector", *stage_free[sb2])
                            for t_ in w_toks2:
                                wait("vector", *t_)
                        wait("vector", "pe", pe_t)
                        emit("vector", lambda E, bk=bk, grp=grp, sb2=sb2, sc=sc: E.tensor_scalar(
                            stage2[sb2][:, grp * 512:(grp + 1) * 512], ps[bk][:, :],
                            w_sl[:, sc:sc + 1], None, OP.mult), inc="ve")
                        bank_free[bk] = ("ve", cnt["ve"])
                    ve_st = cnt["ve"]
                    wait("gpsimd", *sc_last)
                    wait("gpsimd", "ve", ve_st)
                    if e == 0:
                        for t_ in zinit:
                            wait("gpsimd", *t_)
                    name = f"st{dma_idx[1] % NST}"
                    dma_idx[1] += 1
                    emit("gpsimd", lambda E, sb2=sb2, e=e, sc=sc: E.indirect_dma_start(
                        out=acc_s,
                        out_offset=IndirectOffsetOnAxis(ap=idxu_all[:, e * 5 + sc:e * 5 + sc + 1], axis=0),
                        in_=stage2[sb2][:, :], in_offset=None, compute_op=OP.add),
                        inc=name, incn=16)
                    stage_free[sb2] = (name, cnt[name])
                    if sc == 4:
                        sc_last = (name, cnt[name])
                wsl_free = ("ve", cnt["ve"])
                hid_free = ("pe", cnt["pe"])
                dfm_free = ("pe", cnt["pe"])
            barrier()
            A.release()

        A.mark()
        out_acc = A.alloc(8 * LQ)

        A.mark()
        xa = A.alloc(DM)
        aa = A.alloc(DM)
        suma = A.alloc(DM)
        abuf_free = ("pe", 0)
        for tl in range(16):
            wait("sync", *abuf_free)
            lx = dma_in(xa, x_in[tl * 128:(tl + 1) * 128, :])
            toks = [lx]
            if "3" in phases:
                la = dma_in(aa, acc_s[tl * 128:(tl + 1) * 128, :])
                toks.append(la)
            for t_ in toks:
                wait("vector", *t_)
            wait("vector", "pe", cnt["pe"])
            if "3" in phases:
                emit("vector", lambda E: E.tensor_tensor(suma, xa, aa, OP.add), inc="ve")
            else:
                emit("vector", lambda E: E.tensor_copy(suma, xa), inc="ve")
            ve_s = cnt["ve"]
            for grp in range(2):
                bk = grp
                for c4 in range(4):
                    c = grp * 4 + c4
                    transp(bk, suma[:, c * 128:(c + 1) * 128],
                           ps[bk][:, c4 * 128:(c4 + 1) * 128],
                           wait_list=[("ve", ve_s)] if c4 == 0 and grp == 0 else (),
                           bump=(c4 == 3), bank_wait=(c4 == 0))
                pe_t = cnt["pe"]
                wait("vector", "pe", pe_t)
                for c4 in range(4):
                    c = grp * 4 + c4
                    emit("vector", lambda E, c=c, c4=c4, bk=bk, tl=tl:
                         E.tensor_copy(out_acc[:, c * LQ + tl * 128:c * LQ + tl * 128 + 128],
                                       ps[bk][:, c4 * 128:(c4 + 1) * 128]),
                         inc="ve" if c4 == 3 else None)
                bank_free[bk] = ("ve", cnt["ve"])
            abuf_free = ("pe", cnt["pe"])
        barrier()
        A.release()

        if "2" in phases:
            A.mark()
            K_sb = A.alloc(2 * L)
            V_sb = A.alloc(32 * 256)
            q_sb = A.alloc(2 * LQ)
            S_row = A.alloc(4096)
            P_row = A.alloc(4096)
            PT_sb = [A.alloc(512), A.alloc(512)]
            O_sb = A.alloc(256)

            msk_free = [("ve", 0), ("ve", 0)]
            pt_free = [("pe", 0), ("pe", 0)]
            srow_free = ("se", 0)
            prow_free = ("pe", 0)
            osb_free = ("pe", 0)

            for g in range(G):
                wait("sync", "ve", cnt["ve"])
                wait("sync", "pe", cnt["pe"])
                lK = dma_in(K_sb.rearrange("p (c t) -> p c t", c=2),
                            k_s.rearrange("(c p) t -> p c t", p=128)
                            [:, 2 * g:2 * g + 2, :])
                lV = dma_in(V_sb.rearrange("p (c d) -> p c d", c=32),
                            v_s.rearrange("(c p) d -> p c d", p=128)
                            [:, :, g * 256:(g + 1) * 256])
                lq = dma_in(q_sb.rearrange("p (c t) -> p c t", c=2),
                            q_s.rearrange("(c p) t -> p c t", p=128)
                            [:, 2 * g:2 * g + 2, :])
                for qt in range(16):
                    l0 = qt * 128
                    kt0 = l0 // 512
                    kts = list(range(kt0, 8))
                    wait("vector", "ge", cnt["ge"])
                    emit("vector", lambda E, qt=qt, g=g: E.tensor_scalar(
                        S1(THRQ), dxcols[:, qt:qt + 1], WIN[g], None, OP.add), inc="ve")
                    fence("vector")
                    emit("vector", lambda E, qt=qt: E.tensor_tensor(
                        S1(THRQ), S1(THRQ), qcap[:, qt:qt + 1], OP.min), inc="ve")
                    fence("vector")
                    for ki, kt in enumerate(kts):
                        emit("vector", lambda E, ki=ki, off=float(l0 - kt * 512): E.tensor_scalar(
                            S1(THRK + 1 + ki), S1(THRQ), off, None, OP.add),
                            inc="ve" if ki == len(kts) - 1 else None)
                    fence("vector")
                    thrk_ve = cnt["ve"]
                    for ki, kt in enumerate(kts):
                        m0 = kt * 512
                        var = 1 + (l0 - m0) // 128 if kt == kt0 else 0
                        mb = ki % 2
                        ss, vv = msk_free[mb]
                        wait("gpsimd", ss, vv)
                        if ki == 0:
                            wait("gpsimd", "ve", thrk_ve)
                        emit("gpsimd", lambda E, var=var, mb=mb, ki=ki: E.tensor_scalar(
                            msku[mb][:, :], dmats[:, var * 512:(var + 1) * 512],
                            S1(THRK + 1 + ki), None, OP.is_le), inc="ge")
                        msk_ge = cnt["ge"]
                        bk = ki % 2
                        pairs = [(q_sb[:, d * LQ + l0:d * LQ + l0 + 128],
                                  K_sb[:, d * L + m0:d * L + m0 + 512], ps[bk][:, :])
                                 for d in range(2)]
                        pe_s = mm_group(bk, pairs,
                                        wait_list=[lq, lK] if ki == 0 and qt == 0 else ())
                        wait("vector", "pe", pe_s)
                        wait("vector", "ge", msk_ge)
                        if ki == 0:
                            ss, vv = srow_free
                            wait("vector", ss, vv)
                        emit("vector", lambda E, kt=kt, mb=mb, bk=bk: E.select(
                            S_row[:, kt * 512:(kt + 1) * 512], msku[mb][:, :],
                            ps[bk][:, :], negbig[:, :]), inc="ve")
                        msk_free[mb] = ("ve", cnt["ve"])
                        bank_free[bk] = ("ve", cnt["ve"])
                    sel_done = cnt["ve"]
                    if debug and g == 0 and qt == 0:
                        dma_out(dbg2, S_row, waits=[("ve", sel_done)])
                    used0 = kt0 * 512
                    ss, vv = srow_free
                    wait("vector", ss, vv)
                    fence("vector")
                    emit("vector", lambda E, used0=used0:
                         E.reduce_max(S1(MAXV), S_row[:, used0:4096], axis=AX), inc="ve")
                    fence("vector")
                    emit("vector", lambda E: E.tensor_scalar(
                        S1(EBIAS), S1(MAXV), -ESC, None, OP.mult), inc="ve")
                    ve_bias = cnt["ve"]
                    ss, vv = prow_free
                    wait("scalar", ss, vv)
                    wait("scalar", "ve", ve_bias)
                    emit("scalar", lambda E, used0=used0: E.activation(
                        P_row[:, used0:4096], S_row[:, used0:4096], AF.Exp,
                        bias=S1(EBIAS), scale=ESC, accum_out=S1(SUMV)), inc="se")
                    se_exp = cnt["se"]
                    srow_free = ("se", se_exp)
                    if debug and g == 0 and qt == 0:
                        dma_out(dbg3, P_row, waits=[("se", se_exp)])
                        dma_out(dbg[:, 16:48], sm[:, 0:32], waits=[("se", se_exp)])
                    wait("vector", "se", se_exp)
                    fence("vector")
                    emit("vector", lambda E: E.reciprocal(S1(RECIP), S1(SUMV)), inc="ve")
                    ve_recip = cnt["ve"]
                    ss, vv = osb_free
                    first_pv = True
                    for ki, kt in enumerate(kts):
                        tb_ = 2 + ki % 2
                        for c4 in range(4):
                            kc = kt * 4 + c4
                            transp(tb_, P_row[:, kc * 128:(kc + 1) * 128],
                                   ps[tb_][:, c4 * 128:(c4 + 1) * 128],
                                   wait_list=[("se", se_exp)] if c4 == 0 else (),
                                   bump=(c4 == 3), bank_wait=(c4 == 0))
                        pe_t = cnt["pe"]
                        sbw = ki % 2
                        sss, vvv = pt_free[sbw]
                        wait("scalar", "pe", max(pe_t, vvv if sss == "pe" else 0))
                        if sss != "pe":
                            wait("scalar", sss, vvv)
                        emit("scalar", lambda E, tb_=tb_, sbw=sbw:
                             E.copy(PT_sb[sbw], ps[tb_][:, :]), inc="se")
                        se_pt = cnt["se"]
                        bank_free[tb_] = ("se", se_pt)
                        ob = 4 + qt % 2
                        if first_pv:
                            s0, v0 = bank_free[ob]
                            wait("tensor", s0, v0)
                            if qt == 0:
                                wait("tensor", *lV)
                        wait("tensor", "se", se_pt)
                        for c4 in range(4):
                            kc = kt * 4 + c4
                            st_ = first_pv and c4 == 0
                            sp_ = (ki == len(kts) - 1) and c4 == 3
                            emit("tensor", lambda E, c4=c4, kc=kc, ob=ob, st_=st_, sp_=sp_, sbw=sbw:
                                 E.matmul(ps[ob][:, 0:256], PT_sb[sbw][:, c4 * 128:(c4 + 1) * 128],
                                          V_sb[:, kc * 256:(kc + 1) * 256],
                                          start=st_, stop=sp_, skip_group_check=True),
                                 inc="pe" if sp_ else None)
                        first_pv = False
                        pt_free[sbw] = ("pe", cnt["pe"])
                    prow_free = ("pe", cnt["pe"])
                    pe_pv = cnt["pe"]
                    ob = 4 + qt % 2
                    ss, vv = osb_free
                    wait("vector", ss, vv)
                    wait("vector", "pe", pe_pv)
                    wait("vector", "ve", ve_recip)
                    emit("vector", lambda E, ob=ob: E.tensor_scalar(
                        O_sb, ps[ob][:, 0:256], S1(RECIP), None, OP.mult), inc="ve")
                    ve_o = cnt["ve"]
                    bank_free[ob] = ("ve", ve_o)
                    bk = 6
                    for c2 in range(2):
                        transp(bk, O_sb[:, c2 * 128:(c2 + 1) * 128],
                               ps[bk][:, c2 * 128:(c2 + 1) * 128],
                               wait_list=[("ve", ve_o)] if c2 == 0 else (),
                               bump=(c2 == 1), bank_wait=(c2 == 0))
                    osb_free = ("pe", cnt["pe"])
                    wait("vector", "pe", cnt["pe"])
                    for c2 in range(2):
                        fc = 2 * g + c2
                        oa = out_acc[:, fc * LQ + l0:fc * LQ + l0 + 128]
                        emit("vector", lambda E, oa=oa, c2=c2, bk=bk: E.tensor_tensor(
                            oa, oa, ps[bk][:, c2 * 128:(c2 + 1) * 128], OP.add),
                            inc="ve" if c2 == 1 else None)
                    bank_free[bk] = ("ve", cnt["ve"])
            barrier()
            A.release()

        A.mark()
        stage = [A.alloc(DM), A.alloc(DM)]
        stage_free = [("st0", 0), ("st0", 0)]
        for tl in range(16):
            sb = tl % 2
            for grp in range(2):
                bk = grp
                for c4 in range(4):
                    c = grp * 4 + c4
                    transp(bk, out_acc[:, c * LQ + tl * 128:c * LQ + tl * 128 + 128],
                           ps[bk][:, c4 * 128:(c4 + 1) * 128],
                           bump=(c4 == 3), bank_wait=(c4 == 0))
                pe_t = cnt["pe"]
                wait("vector", *stage_free[sb])
                wait("vector", "pe", pe_t)
                for c4 in range(4):
                    c = grp * 4 + c4
                    emit("vector", lambda E, c=c, c4=c4, bk=bk, sb=sb:
                         E.tensor_copy(stage[sb][:, c * 128:(c + 1) * 128],
                                       ps[bk][:, c4 * 128:(c4 + 1) * 128]),
                         inc="ve" if c4 == 3 else None)
                bank_free[bk] = ("ve", cnt["ve"])
            stage_free[sb] = dma_out(out_d[tl * 128:(tl + 1) * 128, :], stage[sb],
                                     waits=[("ve", cnt["ve"])])
        if debug:
            dma_out(dbg[:, 0:16], dxcols)
        A.release()
        A.release()

        for n in sem_names:
            if n.startswith(("st", "ld")):
                wait("gpsimd", n, cnt[n])

        with nc.Block() as block:
            @block.sync
            def _(E):
                for f in ops["sync"]:
                    f(E)

            @block.tensor
            def _(E):
                for f in ops["tensor"]:
                    f(E)

            @block.vector
            def _(E):
                for f in ops["vector"]:
                    f(E)

            @block.scalar
            def _(E):
                for f in ops["scalar"]:
                    f(E)

            @block.gpsimd
            def _(E):
                for f in ops["gpsimd"]:
                    f(E)

    return nc


def make_inputs(x_b, h, Wq, Wqkv, bqkv, Wop, W2, b2, Wr, Wg, Wu, Wd):
    xc = np.ascontiguousarray(np.roll(x_b, -2048 * h, axis=0))
    J0 = 4095 if h == 0 else 2047
    p = np.arange(128)[:, None]
    i = np.arange(16)[None, :]
    qcap = (J0 - (i * 128 + p)).astype(np.float32)
    dmats = np.empty((128, 5 * 512), np.float32)
    j = np.arange(512)[None, :]
    base = (j - p).astype(np.float32)
    dmats[:, 0:512] = base
    for v in range(4):
        r = v * 128
        m = np.where((j - p) < r, np.float32(1e9), base)
        dmats[:, (v + 1) * 512:(v + 2) * 512] = m
    tokid = (np.arange(16)[None, :] * 128 + p).astype(np.float32)
    return {
        "x": xc,
        "tokid": tokid,
        "qcap": qcap,
        "wqt": np.ascontiguousarray(Wq.T),
        "wqkvt": np.ascontiguousarray(Wqkv.T),
        "bqkv_kc": np.ascontiguousarray(bqkv[1024:2048].reshape(8, 128).T),
        "bqkv_qc": np.ascontiguousarray(bqkv[0:1024].reshape(8, 128).T),
        "bqkv_vb": np.ascontiguousarray(np.broadcast_to(bqkv[2048:3072][None, :], (128, 1024))),
        "wopt": np.ascontiguousarray(Wop.T),
        "w2t": np.ascontiguousarray(W2.reshape(2, 128).T),
        "b2c": np.ascontiguousarray(np.broadcast_to(b2.reshape(1, 1), (128, 1))),
        "ident": np.eye(128, dtype=np.float32),
        "dmats": dmats,
        "wr": Wr,
        "wg": Wg,
        "wu": Wu,
        "wd": Wd,
    }


def kernel(L0_attn_x, L0_attn_Wq, L0_attn_Wk, L0_attn_Wv, L0_attn_Wqkv, L0_attn_bqkv,
           L0_attn_Wop, L0_attn_W2, L0_attn_b2, L0_ffn_Wr, L0_ffn_Wg, L0_ffn_Wu,
           L0_ffn_Wd):
    x = np.ascontiguousarray(np.asarray(L0_attn_x, dtype=np.float32))
    args = [np.ascontiguousarray(np.asarray(a, dtype=np.float32)) for a in
            (L0_attn_Wq, L0_attn_Wqkv, L0_attn_bqkv, L0_attn_Wop, L0_attn_W2,
             L0_attn_b2, L0_ffn_Wr, L0_ffn_Wg, L0_ffn_Wu, L0_ffn_Wd)]
    nc = build()
    in_maps = [make_inputs(x[c // 2], c % 2, *args) for c in range(8)]
    res = run_bass_kernel_spmd(nc, in_maps, core_ids=list(range(8)))
    out = np.empty((4, 4096, 1024), np.float32)
    for c in range(8):
        b, h = c // 2, c % 2
        out[b, 2048 * h:2048 * (h + 1), :] = np.asarray(res.results[c]["out"])
    return out
